# version 18
# baseline (speedup 1.0000x reference)
import contextlib
import numpy as np
import ml_dtypes
import concourse.bass as bass
import concourse.mybir as mybir
from concourse.bass_utils import run_bass_kernel_spmd

F32 = mybir.dt.float32
BF16 = mybir.dt.bfloat16
AF = mybir.ActivationFunctionType
ALU = mybir.AluOpType
AX = mybir.AxisListType
BF = ml_dtypes.bfloat16

EXT = 4608
NCH = 36
CH_OWN = 20
D = 2048
DT = 16
HV, HK = 16, 8
DK = DV = 128
NGRP = 9
LN128 = float(np.log(128.0))

TAPS = set()


def _is_ap(x):
    return hasattr(x, "tensor") and hasattr(x, "ap") and hasattr(x, "offset")


def _rng(apx):
    t = apx.tensor
    name = t.name
    tn = type(t).__name__
    if "PSum" in tn:
        return (name, 0, 1 << 60)
    if "SBTensor" in tn:
        F = int(t.shape[1])
        off = int(apx.offset)
        lo = hi = off % F
        ok = True
        for pair in apx.ap:
            st, cnt = int(pair[0]), int(pair[1])
            if cnt <= 1:
                continue
            if st % F == 0:
                continue
            if abs(st) >= F:
                ok = False
                break
            d = st * (cnt - 1)
            if d >= 0:
                hi += d
            else:
                lo += d
        if ok and 0 <= lo and hi < F:
            return (name, lo, hi + 1)
        return (name, 0, F)
    off = int(apx.offset)
    lo = hi = off
    for pair in apx.ap:
        st, cnt = int(pair[0]), int(pair[1])
        if cnt > 1:
            d = st * (cnt - 1)
            if d >= 0:
                hi += d
            else:
                lo += d
    return (name, lo, hi + 1)


def _fps(fn, a, k):
    W, R = [], []
    def add(lst, x):
        if _is_ap(x):
            lst.append(x)
    if fn == "dma_start":
        add(W, a[0]); add(R, a[1])
    elif fn == "matmul":
        add(W, a[0]); add(R, a[0]); add(R, a[1]); add(R, a[2])
    elif fn == "transpose":
        add(W, a[0]); add(R, a[1]); add(R, a[2])
    elif fn in ("tensor_copy", "copy", "reciprocal", "tensor_reduce", "activation"):
        add(W, a[0]); add(R, a[1])
        for key in ("bias", "scale"):
            if key in k:
                add(R, k[key])
    elif fn in ("tensor_tensor", "tensor_tensor_scan"):
        add(W, a[0]); add(R, a[1]); add(R, a[2])
        if fn == "tensor_tensor_scan" and len(a) > 3:
            add(R, a[3])
    elif fn in ("tensor_scalar", "scalar_tensor_tensor"):
        add(W, a[0]); add(R, a[1]); add(R, a[2])
        if len(a) > 3:
            add(R, a[3])
    elif fn in ("memset", "iota"):
        add(W, a[0])
    else:
        raise AssertionError("footprint: unknown fn " + fn)
    return W, R


_STREAM_OF = {("tensor", False): "T", ("vector", False): "V",
              ("scalar", False): "A", ("gpsimd", False): "G",
              ("sync", True): "SD", ("gpsimd", True): "GD",
              ("scalar", True): "XD", ("vector", True): "VD"}


class Prog:

    EPS_C = 30000
    EPS_D = 3000

    def __init__(self):
        self.ops = []
        self.cbias = None
        self.bias_cols = {}

    def op(self, eng, fn, *a, **k):
        self.ops.append((eng, fn, a, k, False))

    def dma(self, eng, out, in_, **k):
        self.ops.append((eng, "dma_start", (out, in_), k, True))

    def fence(self, eng):
        self.ops.append((eng, "__fence__", (), {}, True))

    def barrier(self):
        self.ops.append(("sync", "__barrier__", (), {}, False))

    def mm(self, out, lhsT, rhs, start=True, stop=True):
        self.op("tensor", "matmul", out, lhsT, rhs, start=start, stop=stop)

    def tr(self, out, in_, ident):
        self.op("tensor", "transpose", out, in_, ident)

    def cp(self, out, in_, eng="vector"):
        self.op(eng, "copy" if eng == "scalar" else "tensor_copy", out, in_)

    def tt(self, out, a, b, alu, eng="vector"):
        self.op(eng, "tensor_tensor", out, a, b, alu)

    def ts(self, out, in_, s1, s2, op0, op1=ALU.bypass, eng="vector"):
        self.op(eng, "tensor_scalar", out, in_, s1, s2, op0, op1)

    def stt(self, out, in0, scalar, in1, op0, op1, eng="vector"):
        self.op(eng, "scalar_tensor_tensor", out, in0, scalar, in1, op0, op1)

    def act(self, out, in_, fn, bias=0.0, scale=1.0):
        if isinstance(bias, float) and bias != 0.0 and self.cbias is not None:
            if bias not in self.bias_cols:
                col = len(self.bias_cols)
                self.bias_cols[bias] = col
                self.op("vector", "memset", self.cbias[:, col:col + 1], bias)
            col = self.bias_cols[bias]
            rows = out.shape[0]
            bias = self.cbias[0:rows, col:col + 1]
        self.op("scalar", "activation", out, in_, fn, bias=bias, scale=scale)

    def red(self, out, in_, alu, axis=AX.X):
        self.op("vector", "tensor_reduce", out, in_, axis, alu)


    def _schedule(self):
        ops = self.ops
        cnt = {s: 0 for s in set(_STREAM_OF.values())}
        qbound = {s: 0 for s in cnt}
        pending_gate = {s: None for s in cnt}
        wm = {}
        log = {}
        sched = []
        dma_streams = {"SD", "GD", "XD", "VD"}

        floor = {s: 0 for s in cnt}

        def close(q, c=None):
            v = cnt[q] if c is None else c
            if v > qbound[q]:
                qbound[q] = v
                pending_gate[q] = v

        for i, (en, fn, a, k, isdma) in enumerate(ops):
            if fn == "__fence__":
                close(_STREAM_OF[(en, True)])
                sched.append(None)
                continue
            if fn == "__barrier__":
                for s in cnt:
                    floor[s] = cnt[s]
                    if s in dma_streams:
                        close(s)
                sched.append(None)
                continue
            st = _STREAM_OF[(en, isdma)]
            W, R = _fps(fn, a, k)
            deps = {s: f for s, f in floor.items() if f > 0}
            entries = []
            for x in W:
                entries.append((_rng(x), True))
            for x in R:
                entries.append((_rng(x), False))
            for (name, lo, hi), iw in entries:
                ll = log.get(name)
                if not ll:
                    continue
                for (s2, c2, lo2, hi2, w2) in ll:
                    if (iw or w2) and lo < hi2 and lo2 < hi:
                        if deps.get(s2, 0) < c2:
                            deps[s2] = c2
            waits = []
            for s2, c2 in deps.items():
                if s2 == st and st == "T":
                    continue
                if s2 in dma_streams:
                    if c2 > qbound[s2]:
                        close(s2)
                    req = qbound[s2]
                else:
                    req = c2
                key = (st, s2)
                if wm.get(key, 0) < req:
                    wm[key] = req
                    waits.append((s2, req))
            gate = None
            if isdma:
                q = cnt[st]
                if q > 0 and q % Prog.EPS_D == 0:
                    close(st)
                if pending_gate[st] is not None:
                    gate = pending_gate[st]
                    pending_gate[st] = None
                    key = (st, st)
                    if wm.get(key, 0) >= gate:
                        gate = None
                    else:
                        wm[key] = gate
            cnt[st] += 1
            c = cnt[st]
            for (name, lo, hi), iw in entries:
                ll = log.setdefault(name, [])
                ll.append((st, c, lo, hi, iw))
                if len(ll) > 800:
                    best = {}
                    for (s2, c2, _, _, w2) in ll:
                        if best.get(s2, 0) < c2:
                            best[s2] = c2
                    log[name] = [(s2, c2, 0, 1 << 60, True)
                                 for s2, c2 in best.items()]
            sched.append((st, c, waits, gate))
        return sched, cnt

    def emit(self, nc, blk, stack):
        sched, totals = self._schedule()
        dma_streams = {"SD", "GD", "XD", "VD"}
        sems = {}
        for s, tot in totals.items():
            if tot == 0:
                continue
            eps = Prog.EPS_D if s in dma_streams else Prog.EPS_C
            neps = (tot + eps - 1) // eps
            sems[s] = [stack.enter_context(nc.semaphore(f"q_{s}{e}"))
                       for e in range(neps)]

        def sem_val(s, c):
            eps = Prog.EPS_D if s in dma_streams else Prog.EPS_C
            e = (c - 1) // eps
            v = c - e * eps
            if s in dma_streams:
                v *= 16
            return sems[s][e], v

        ops = self.ops
        eng_items = {}
        for i, p in enumerate(ops):
            if p[1] in ("__fence__", "__barrier__"):
                continue
            eng_items.setdefault(p[0], []).append(i)

        def runner(idxs):
            def run(eng):
                for i in idxs:
                    en, fn, a, k, isdma = ops[i]
                    st, c, waits, gate = sched[i]
                    if gate is not None:
                        sm, v = sem_val(st, gate)
                        eng.wait_ge(sm, v)
                    for s2, c2 in waits:
                        sm, v = sem_val(s2, c2)
                        eng.wait_ge(sm, v)
                    inst = getattr(eng, fn)(*a, **k)
                    sm, v = sem_val(st, c)
                    inst.then_inc(sm, 16 if isdma else 1)
                for s, tot in totals.items():
                    if tot:
                        sm, v = sem_val(s, tot)
                        eng.wait_ge(sm, v)
            return run

        def idle(eng):
            for s, tot in totals.items():
                if tot:
                    sm, v = sem_val(s, tot)
                    eng.wait_ge(sm, v)

        for name in ["sync", "scalar", "vector", "tensor", "gpsimd"]:
            idxs = eng_items.get(name, [])
            blk_fn = getattr(blk, name)
            if idxs:
                blk_fn(runner(idxs))
            else:
                blk_fn(idle)


def build_nc():
    nc = bass.Bass("TRN2", target_bir_lowering=False, debug=False)
    ap = lambda t: t.ap()

    xe = ap(nc.dram_tensor("xe", [EXT, D], F32, kind="ExternalInput"))
    mh = ap(nc.dram_tensor("mh", [16, EXT], F32, kind="ExternalInput"))
    wcat = ap(nc.dram_tensor("wcat", [D, 6176], F32, kind="ExternalInput"))
    convd = ap(nc.dram_tensor("convd", [128, 4 * 32 * 128], BF16, kind="ExternalInput"))
    alog = ap(nc.dram_tensor("alog", [16, 1], F32, kind="ExternalInput"))
    dtb = ap(nc.dram_tensor("dtb", [16, 1], F32, kind="ExternalInput"))
    nwbi = ap(nc.dram_tensor("nwbi", [128, D], BF16, kind="ExternalInput"))
    wout = ap(nc.dram_tensor("wout", [D, D], F32, kind="ExternalInput"))
    gwt = ap(nc.dram_tensor("gwt", [D, 16], F32, kind="ExternalInput"))
    w1t = ap(nc.dram_tensor("w1t", [16 * D, 1408], F32, kind="ExternalInput"))
    w3t = ap(nc.dram_tensor("w3t", [16 * D, 1408], F32, kind="ExternalInput"))
    w2t = ap(nc.dram_tensor("w2t", [16 * 1408, D], F32, kind="ExternalInput"))
    ws1t = ap(nc.dram_tensor("ws1t", [D, 2816], F32, kind="ExternalInput"))
    ws3t = ap(nc.dram_tensor("ws3t", [D, 2816], F32, kind="ExternalInput"))
    ws2t = ap(nc.dram_tensor("ws2t", [2816, D], F32, kind="ExternalInput"))
    idb = ap(nc.dram_tensor("idb", [128, 128], BF16, kind="ExternalInput"))
    idf = ap(nc.dram_tensor("idf", [128, 128], F32, kind="ExternalInput"))
    sel16 = ap(nc.dram_tensor("sel16", [16, 2048], F32, kind="ExternalInput"))
    onek = ap(nc.dram_tensor("onek", [128, 128], BF16, kind="ExternalInput"))
    iotas = ap(nc.dram_tensor("iotas", [128, 40], F32, kind="ExternalInput"))
    masks = ap(nc.dram_tensor("masks", [128, 256], BF16, kind="ExternalInput"))

    yout = ap(nc.dram_tensor("yout", [D, D], F32, kind="ExternalOutput"))

    lnfm_d = nc.dram_tensor("lnfm_d", [DT, 128, EXT], BF16, kind="Internal")
    qkv_d = nc.dram_tensor("qkv_d", [32, 128, EXT], BF16, kind="Internal")
    z_d = nc.dram_tensor("z_d", [D, D], BF16, kind="Internal")
    ofm_d = nc.dram_tensor("ofm_d", [DT, 128, D], BF16, kind="Internal")
    x1_d = nc.dram_tensor("x1_d", [D, D], F32, kind="Internal")
    l2fm_d = nc.dram_tensor("l2fm_d", [DT, 128, D], BF16, kind="Internal")
    comb_d = nc.dram_tensor("comb_d", [128, 256], F32, kind="Internal")
    wffd13 = nc.dram_tensor("wffd13", [576 * 128, 1408], BF16, kind="Internal")
    wffd2 = nc.dram_tensor("wffd2", [198 * 128, 2048], BF16, kind="Internal")
    dbg_d = nc.dram_tensor("dbg_d", [128, 64], F32, kind="Internal")
    scratch = dict(lnfm_d=lnfm_d, qkv_d=qkv_d, z_d=z_d, ofm_d=ofm_d,
                   x1_d=x1_d, l2fm_d=l2fm_d, comb_d=comb_d, dbg_d=dbg_d)

    P = Prog()
    outer = contextlib.ExitStack()
    sbo = lambda n_, s, dt_: outer.enter_context(nc.sbuf_tensor(n_, s, dt_))
    comb = sbo("comb", [128, 16 * 16], F32)
    c_idb = sbo("c_idb", [128, 128], BF16)
    c_idf = sbo("c_idf", [128, 128], F32)
    c_iota = sbo("c_iota", [128, 40], F32)
    P.cbias = sbo("cbias", [128, 8], F32)
    P.dma("sync", c_idb[:, :], idb)
    P.dma("sync", c_idf[:, :], idf)
    P.dma("sync", c_iota[:, :], iotas)

    stg = sbo("stg", [128, 2 * 2048], BF16)

    _prep_jobs = []
    for e in range(18):
        for m in range(2):
            for kt in range(DT):
                blk = (e * 2 + m) * DT + kt
                if e < 16:
                    t = w1t if m == 0 else w3t
                    s = t[e * D + kt * 128:e * D + (kt + 1) * 128, :]
                else:
                    t = ws1t if m == 0 else ws3t
                    sh = e - 16
                    s = t[kt * 128:(kt + 1) * 128, sh * 1408:(sh + 1) * 1408]
                _prep_jobs.append((s, wffd13.ap()[blk * 128:(blk + 1) * 128, :], 1408))
        for it in range(11):
            gi = e * 11 + it
            if e < 16:
                s = w2t[e * 1408 + it * 128: e * 1408 + (it + 1) * 128, :]
            else:
                sh = e - 16
                s = ws2t[sh * 1408 + it * 128: sh * 1408 + (it + 1) * 128, :]
            _prep_jobs.append((s, wffd2.ap()[gi * 128:(gi + 1) * 128, :], 2048))
    _prep_jobs.reverse()

    def prep_drain(n):
        for _ in range(n):
            if not _prep_jobs:
                return
            srcap, dstap, wdt = _prep_jobs.pop()
            par = len(_prep_jobs) % 2
            sgv = stg[:, par * 2048:par * 2048 + wdt]
            P.dma("gpsimd", sgv, srcap)
            P.fence("gpsimd")
            P.dma("gpsimd", dstap, sgv)
            P.fence("gpsimd")

    sAB = contextlib.ExitStack()
    sab = lambda n_, s, dt_: sAB.enter_context(nc.sbuf_tensor(n_, s, dt_))
    gml = sab("gml", [16, EXT], F32)[:, :]
    lnbm = sab("lnbm", [16, EXT], F32)[:, :]
    betm = sab("betm", [16, EXT], F32)[:, :]
    sBA = contextlib.ExitStack()
    bab = sBA.enter_context(nc.sbuf_tensor("bab", [16, EXT], F32))[:, :]
    baa = sBA.enter_context(nc.sbuf_tensor("baa", [16, EXT], F32))[:, :]

    sA = contextlib.ExitStack()
    sa = lambda n_, s, dt_: sA.enter_context(nc.sbuf_tensor(n_, s, dt_))
    a_x2 = sa("a_x2", [128, 2 * D], F32)
    a_sq = sa("a_sq", [128, D], F32)
    a_xc = sa("a_xc", [128, D], F32)
    a_sc = sa("a_sc", [128, 4], F32)
    a_ln = sa("a_ln", [128, D], BF16)
    a_fm2 = sa("a_fm2", [128, 2 * D], BF16)
    a_pt = sA.enter_context(nc.psum_tensor("a_pt", [128, 1024], BF16))
    P.dma("sync", a_x2[:, 0:D], xe[0:128, :])
    P.fence("sync")
    for ch in range(NCH):
        a_x = a_x2[:, (ch % 2) * D:(ch % 2) * D + D]
        a_fm = a_fm2[:, (ch % 2) * D:(ch % 2) * D + D]
        if ch + 1 < NCH:
            P.dma("sync", a_x2[:, ((ch + 1) % 2) * D:((ch + 1) % 2) * D + D],
                  xe[(ch + 1) * 128:(ch + 2) * 128, :])
            P.fence("sync")
        prep_drain(4)
        P.red(a_sc[:, 0:1], a_x, ALU.add)
        P.ts(a_sc[:, 1:2], a_sc[:, 0:1], 1.0 / D, None, ALU.mult)
        P.ts(a_xc[:, :], a_x, a_sc[:, 1:2], None, ALU.subtract)
        P.op("scalar", "activation", a_sq[:, :], a_xc[:, :], AF.Square)
        P.red(a_sc[:, 2:3], a_sq[:, :], ALU.add)
        P.act(a_sc[:, 3:4], a_sc[:, 2:3], AF.Ln, bias=1e-5, scale=1.0 / D)
        P.act(a_sc[:, 3:4], a_sc[:, 3:4], AF.Exp, scale=-0.5)
        P.ts(a_ln[:, :], a_xc[:, :], a_sc[:, 3:4], None, ALU.mult)
        for t in range(0, 16, 8):
            for dt2 in range(t, t + 8):
                P.tr(a_pt[:, (dt2 - t) * 128:(dt2 - t + 1) * 128],
                     a_ln[:, dt2 * 128:(dt2 + 1) * 128], c_idb[:, :])
            P.cp(a_fm[:, t * 128:(t + 8) * 128], a_pt[:, :])
        P.dma("scalar",
              lnfm_d.ap().rearrange("d p t -> p d t")[:, :, ch * 128:(ch + 1) * 128],
              a_fm.rearrange("p (d t) -> p d t", t=128))
        P.fence("scalar")
    P.barrier()
    sA.close()

    QCOL = [0, 1536, 3072, 4608, 6176]
    sA1 = contextlib.ExitStack()
    sa1 = lambda n_, s, dt_: sA1.enter_context(nc.sbuf_tensor(n_, s, dt_))
    pa1 = lambda n_, s, dt_: sA1.enter_context(nc.psum_tensor(n_, s, dt_))
    a_w = sa1("a_w", [128, DT * 1568], BF16)
    a_lg2 = sa1("a_lg2", [128, 2 * DT * 512], BF16)
    a_ob = sa1("a_ob", [128, 12 * 512], BF16)
    a_zb = sa1("a_zb", [128, 512], BF16)
    a_p0 = pa1("a_p0", [128, 512], F32)
    a_p1 = pa1("a_p1", [128, 512], F32)
    a_pb = pa1("a_pb", [16, 512], F32)
    for q in range(4):
        c0, c1 = QCOL[q], QCOL[q + 1]
        W = c1 - c0
        for dt2 in range(DT):
            P.dma("gpsimd", a_w[:, dt2 * W:dt2 * W + W],
                  wcat[dt2 * 128:(dt2 + 1) * 128, c0:c1])
        for grp in range(NGRP):
            g0 = grp * 512
            ct_lo, ct_hi = c0 // 128, min(c1, 4096) // 128
            cts = [c for c in range(ct_lo, ct_hi)
                   if not (grp <= 3 and c < 8)]
            do_z = (grp >= 5 and q >= 2)
            do_ba = (q == 3)
            if not cts and not do_z and not do_ba:
                continue
            prep_drain(6)
            par = grp % 2
            a_lg = a_lg2[:, par * DT * 512:(par + 1) * DT * 512]
            P.dma("sync", a_lg.rearrange("p (d t) -> p d t", t=512),
                  lnfm_d.ap().rearrange("d p t -> p d t")[:, :, g0:g0 + 512])
            P.fence("sync")
            pp = [a_p0, a_p1]
            for ci, ct in enumerate(cts):
                ps = pp[ci % 2]
                off = ct * 128 - c0
                for dt2 in range(DT):
                    P.mm(ps[:, :], a_w[:, dt2 * W + off:dt2 * W + off + 128],
                         a_lg[:, dt2 * 512:(dt2 + 1) * 512],
                         start=(dt2 == 0), stop=(dt2 == DT - 1))
                P.cp(a_ob[:, ci * 512:(ci + 1) * 512], ps[:, :])
            if cts:
                P.dma("scalar",
                      qkv_d.ap().rearrange("c p t -> p c t")[:, cts[0]:cts[0] + len(cts), g0:g0 + 512],
                      a_ob[:, 0:len(cts) * 512].rearrange("p (c t) -> p c t", t=512))
                P.fence("scalar")
            if do_ba:
                for dt2 in range(DT):
                    P.mm(a_pb[:, :], a_w[:, dt2 * W + 1536:dt2 * W + 1552],
                         a_lg[:, dt2 * 512:(dt2 + 1) * 512],
                         start=(dt2 == 0), stop=(dt2 == DT - 1))
                P.cp(bab[:, g0:g0 + 512], a_pb[:, :])
                for dt2 in range(DT):
                    P.mm(a_pb[:, :], a_w[:, dt2 * W + 1552:dt2 * W + 1568],
                         a_lg[:, dt2 * 512:(dt2 + 1) * 512],
                         start=(dt2 == 0), stop=(dt2 == DT - 1))
                P.cp(baa[:, g0:g0 + 512], a_pb[:, :])
            if do_z:
                nbs = [0] if q == 2 else [1, 2, 3]
                for tc in range(4):
                    for nb in nbs:
                        off = 4096 + nb * 512 - c0
                        ps = pp[nb % 2]
                        for dt2 in range(DT):
                            P.mm(ps[:, :],
                                 a_lg[:, dt2 * 512 + tc * 128:dt2 * 512 + tc * 128 + 128],
                                 a_w[:, dt2 * W + off:dt2 * W + off + 512],
                                 start=(dt2 == 0), stop=(dt2 == DT - 1))
                        P.cp(a_zb[:, :], ps[:, :])
                        trow = g0 + tc * 128 - 2560
                        P.dma("scalar",
                              z_d.ap()[trow:trow + 128, nb * 512:(nb + 1) * 512],
                              a_zb[:, :])
    P.barrier()
    sA1.close()

    sBp = contextlib.ExitStack()
    sbp = lambda n_, s, dt_: sBp.enter_context(nc.sbuf_tensor(n_, s, dt_))
    mhs = sbp("mhs", [16, EXT], F32)[:, :]
    mneg = sbp("mneg", [16, EXT], F32)[:, :]
    b_t0 = sbp("b_t0", [16, EXT], F32)[:, :]
    b_t1 = sbp("b_t1", [16, EXT], F32)[:, :]
    expa = sbp("expa", [16, 2], F32)
    P.dma("sync", mhs, mh)
    P.ts(mneg, mhs, 1e4, -1e4, ALU.mult, ALU.add)
    P.act(b_t0, bab, AF.Exp, scale=-1.0)
    P.ts(b_t1, b_t0, 1.0, None, ALU.add)
    P.op("vector", "reciprocal", betm, b_t1)
    P.tt(betm, betm, mhs, ALU.mult)
    P.act(b_t0, bab, AF.Abs)
    P.act(b_t0, b_t0, AF.Exp, scale=-1.0)
    P.act(b_t0, b_t0, AF.Ln, bias=1.0)
    P.ts(b_t1, bab, -1.0, 0.0, ALU.mult, ALU.max)
    P.tt(b_t0, b_t0, b_t1, ALU.add)
    P.tt(b_t0, b_t0, mhs, ALU.mult)
    P.stt(lnbm, b_t0, -1.0, mneg, ALU.mult, ALU.add)
    P.dma("sync", expa[:, 0:1], alog)
    P.act(expa[:, 0:1], expa[:, 0:1], AF.Exp)
    P.dma("sync", expa[:, 1:2], dtb)
    P.ts(b_t0, baa, expa[:, 1:2], None, ALU.add)
    P.act(b_t1, b_t0, AF.Abs)
    P.act(b_t1, b_t1, AF.Exp, scale=-1.0)
    P.act(b_t1, b_t1, AF.Ln, bias=1.0)
    P.ts(b_t0, b_t0, 1.0, 0.0, ALU.mult, ALU.max)
    P.tt(b_t0, b_t0, b_t1, ALU.add)
    P.ts(gml, b_t0, expa[:, 0:1], -1.0, ALU.mult, ALU.mult)
    P.tt(gml, gml, mhs, ALU.mult)
    P.barrier()
    sBp.close()
    sBA.close()

    sB = contextlib.ExitStack()
    sb = lambda n_, s, dt_: sB.enter_context(nc.sbuf_tensor(n_, s, dt_))
    pb = lambda n_, s, dt_: sB.enter_context(nc.psum_tensor(n_, s, dt_))
    c_sel = sb("c_sel", [16, 2048], F32)
    c_onek = sb("c_onek", [128, 128], BF16)
    c_mask = sb("c_mask", [128, 256], BF16)
    c_nwb = sb("c_nwb", [128, D], BF16)
    ones16 = sb("ones16", [16, 128], F32)
    cdg = sb("cdg", [128, 4 * 32 * 128], BF16)
    qg = sb("qg", [128, 32 * 259], BF16)
    cvo = sb("cvo", [128, 32 * 256], BF16)
    zsn = sb("zsn", [128, 2 * D], BF16)
    lnkq = sb("lnkq", [16, 256], F32)
    compb = sb("compb", [16, 8 * 128], F32)
    colc = sb("colc", [128, 96], F32)
    ecol = sb("ecol", [128, 64], F32)
    gkk2 = sb("gkk2", [128, 2 * 128], BF16)
    gkq2 = sb("gkq2", [128, 2 * 128], BF16)
    ktk2 = sb("ktk2", [128, 2 * 128], BF16)
    dam2 = sb("dam2", [128, 2 * 256], F32)
    eDb2 = sb("eDb2", [128, 2 * 256], BF16)
    Amat2 = sb("Amat2", [128, 2 * 128], BF16)
    ATmat2 = sb("ATmat2", [128, 2 * 128], BF16)
    MTmat2 = sb("MTmat2", [128, 2 * 128], BF16)
    pw2 = sb("pw2", [128, 2 * 512], BF16)
    Eb2 = sb("Eb2", [128, 2 * 256], BF16)
    ub2 = sb("ub2", [128, 2 * 128], BF16)
    ksb2 = sb("ksb2", [128, 2 * 128], BF16)
    wtb2 = sb("wtb2", [128, 2 * 128], BF16)
    ybb2 = sb("ybb2", [128, 2 * 128], BF16)
    dbb2 = sb("dbb2", [128, 2 * 128], BF16)
    bvb2 = sb("bvb2", [128, 2 * 128], BF16)
    Sbb2 = sb("Sbb2", [128, 2 * 128], BF16)
    ob = sb("ob", [128, HV * DV], F32)
    osq = sb("osq", [128, HV * DV], F32)
    o2 = sb("o2", [128, HV * DV], BF16)
    ofmb = sb("ofmb", [128, D], BF16)
    rsd = sb("rsd", [128, 16], F32)
    S1 = sb("S1", [128, HV * DV], F32)
    S2 = sb("S2", [128, HV * DV], F32)
    p_c0 = pb("p_c0", [128, 512], F32)
    p_d = pb("p_d", [128, 256], F32)
    p_g = pb("p_g", [128, 256], F32)
    p_A0 = pb("p_A0", [128, 512], F32)
    p_A1 = pb("p_A1", [128, 512], F32)
    p_t0 = pb("p_t0", [128, 1024], BF16)
    p_t1 = pb("p_t1", [128, 1024], BF16)
    p_kq = pb("p_kq", [16, 256], F32)

    P.dma("sync", c_sel[:, :], sel16)
    P.dma("sync", c_onek[:, :], onek)
    P.dma("sync", c_mask[:, :], masks)
    P.dma("sync", c_nwb[:, :], nwbi)
    P.op("vector", "memset", ones16[:, :], 1.0)
    P.dma("sync", cdg[:, :], convd)
    P.op("vector", "memset", S1[:, :], 0.0)

    for pair in range(2, 18):
        prep_drain(30)
        b0 = pair * 256
        cts = list(range(32)) if pair >= 9 else list(range(8, 32))
        P.dma("sync", qg[:, :].rearrange("p (c t) -> p c t", t=259),
              qkv_d.ap().rearrange("c p t -> p c t")[:, :, b0 - 3:b0 + 256])
        P.fence("sync")
        for ci, ct in enumerate(cts):
            pcv = [p_c0, p_d][ci % 2]
            for j in range(4):
                P.mm(pcv[:, 0:256], cdg[:, (j * 32 + ct) * 128:(j * 32 + ct + 1) * 128],
                     qg[:, ct * 259 + j:ct * 259 + j + 256],
                     start=(j == 0), stop=(j == 3))
            P.act(cvo[:, ct * 256:(ct + 1) * 256], pcv[:, 0:256], AF.Silu)
        if pair >= 10:
            for lc in range(2):
                trow = b0 + lc * 128 - 2560
                P.dma("sync", zsn[:, lc * D:(lc + 1) * D], z_d.ap()[trow:trow + 128, :])
            for lc in range(2):
                P.act(zsn[:, lc * D:(lc + 1) * D], zsn[:, lc * D:(lc + 1) * D], AF.Silu)
                P.tt(zsn[:, lc * D:(lc + 1) * D], zsn[:, lc * D:(lc + 1) * D],
                     c_nwb[:, :], ALU.mult)
        for lc in range(2):
            ch = pair * 2 + lc
            own = ch >= CH_OWN
            t0 = ch * 128
            lo = lc * 128
            qf = lambda j, lo=lo: cvo[:, j * 256 + lo:j * 256 + lo + 128]
            kf = lambda j, lo=lo: cvo[:, (8 + j) * 256 + lo:(8 + j) * 256 + lo + 128]
            vf = lambda i, lo=lo: cvo[:, (16 + i) * 256 + lo:(16 + i) * 256 + lo + 128]
            P.op("scalar", "activation",
                 o2[:, 0:1024].rearrange("p (c t) -> p c t", t=128),
                 cvo[:, :].rearrange("p (c t) -> p c t", t=256)[:, 8:16, lo:lo + 128],
                 AF.Square)
            for j in range(8):
                P.mm(p_kq[:, 0:128], c_onek[:, j * 16:(j + 1) * 16],
                     o2[:, j * 128:(j + 1) * 128],
                     start=(j == 0), stop=(j == 7))
            P.act(lnkq[:, 0:128], p_kq[:, 0:128], AF.Ln, bias=1e-6)
            if own:
                P.op("scalar", "activation",
                     o2[:, 1024:2048].rearrange("p (c t) -> p c t", t=128),
                     cvo[:, :].rearrange("p (c t) -> p c t", t=256)[:, 0:8, lo:lo + 128],
                     AF.Square)
                for j in range(8):
                    P.mm(p_kq[:, 128:256], c_onek[:, j * 16:(j + 1) * 16],
                         o2[:, 1024 + j * 128:1024 + (j + 1) * 128],
                         start=(j == 0), stop=(j == 7))
                P.act(lnkq[:, 128:256], p_kq[:, 128:256], AF.Ln, bias=1e-6)
            G = compb[:, 7 * 128:8 * 128]
            P.op("vector", "tensor_tensor_scan", G, ones16[:, :],
                 gml[:, t0:t0 + 128], 0.0, ALU.mult, ALU.add)
            R0, R1, R2, R3, R4, Rb, R1M = (compb[:, s * 128:(s + 1) * 128]
                                           for s in range(7))
            P.tt(R0, G, lnbm[:, t0:t0 + 128], ALU.add)
            P.stt(R0, lnkq[:, 0:128], -0.5, R0, ALU.mult, ALU.add)
            P.stt(R1, lnkq[:, 0:128], -0.5, G, ALU.mult, ALU.subtract)
            if own:
                P.stt(R2, lnkq[:, 128:256], -0.5, G, ALU.mult, ALU.add)
            P.ts(R1M, R1, -0.5 * LN128, None, ALU.add)
            P.ts(R3, R1, G[:, 127:128], None, ALU.add)
            P.ts(R4, G, 0.0, G[:, 127:128], ALU.mult, ALU.add)
            P.cp(Rb, betm[:, t0:t0 + 128])
            slots = [0, 2, 3, 4, 6, 5]
            for si, s in enumerate(slots):
                P.tr(p_d[:, si * 16:(si + 1) * 16],
                     compb[:, s * 128:(s + 1) * 128], c_idf[0:16, 0:16])
            P.cp(colc[:, :], p_d[:, 0:96])
            P.act(ecol[:, :], colc[:, 0:64], AF.Exp)
            for i in range(HV):
                j = i // 2
                Pv = i % 2
                Pp = (i // 2) % 2
                ce = "scalar" if Pv else "vector"
                gkk = gkk2[:, Pp * 128:(Pp + 1) * 128]
                gkq = gkq2[:, Pp * 128:(Pp + 1) * 128]
                ktk = ktk2[:, Pp * 128:(Pp + 1) * 128]
                dam = dam2[:, Pv * 256:(Pv + 1) * 256]
                eDb = eDb2[:, Pv * 256:(Pv + 1) * 256]
                Amat = Amat2[:, Pv * 128:(Pv + 1) * 128]
                ATmat = ATmat2[:, Pv * 128:(Pv + 1) * 128]
                MTmat = MTmat2[:, Pv * 128:(Pv + 1) * 128]
                pw = pw2[:, Pv * 512:(Pv + 1) * 512]
                Eb = Eb2[:, Pv * 256:(Pv + 1) * 256]
                ub = ub2[:, Pv * 128:(Pv + 1) * 128]
                ksb = ksb2[:, Pv * 128:(Pv + 1) * 128]
                wtb = wtb2[:, Pv * 128:(Pv + 1) * 128]
                ybb = ybb2[:, Pv * 128:(Pv + 1) * 128]
                dbb = dbb2[:, Pv * 128:(Pv + 1) * 128]
                bvb = bvb2[:, Pv * 128:(Pv + 1) * 128]
                Sbb = Sbb2[:, Pv * 128:(Pv + 1) * 128]
                pA = [p_A0, p_A1][Pv]
                p_t = [p_t0, p_t1][Pv]
                if i % 2 == 0:
                    P.mm(p_g[:, 0:128], kf(j), kf(j))
                    P.cp(gkk, p_g[:, 0:128], eng=ce)
                    P.tt(gkk, gkk, c_mask[:, 0:128], ALU.mult)
                    if own:
                        P.mm(p_g[:, 128:256], kf(j), qf(j))
                        P.cp(gkq, p_g[:, 128:256], eng=ce)
                        P.tt(gkq, gkq, c_mask[:, 128:256], ALU.mult)
                    P.tr(p_t[:, 512:640], kf(j), c_idb[:, :])
                    P.cp(ktk, p_t[:, 512:640], eng=ce)
                P.mm(p_d[:, 0:128], c_sel[:, i * 128:(i + 1) * 128],
                     compb[:, 1 * 128:2 * 128])
                if own:
                    P.mm(p_d[:, 128:256], c_sel[:, i * 128:(i + 1) * 128],
                         compb[:, 2 * 128:3 * 128])
                P.ts(dam[:, 0:128], p_d[:, 0:128], colc[:, 0 + i:1 + i],
                     60.0, ALU.add, ALU.min)
                if own:
                    P.ts(dam[:, 128:256], p_d[:, 128:256], colc[:, 64 + i:65 + i],
                         60.0, ALU.add, ALU.min)
                    P.act(eDb, dam, AF.Exp)
                    P.tt(MTmat, eDb[:, 128:256], gkq, ALU.mult)
                else:
                    P.act(eDb[:, 0:128], dam[:, 0:128], AF.Exp)
                P.tt(Amat, eDb[:, 0:128], gkk, ALU.mult)
                P.tr(p_t[:, 640:768], Amat, c_idb[:, :])
                P.cp(ATmat, p_t[:, 640:768], eng=ce)
                P.stt(Eb[:, 0:128], ATmat, -1.0, c_idb[:, :],
                      ALU.mult, ALU.add)
                ecur = 0
                pcur, ptcur = Amat, ATmat
                for lev in range(1, 7):
                    P.mm(pA[:, 0:128], ptcur, pcur)
                    if lev < 6:
                        P.mm(pA[:, 128:256], pcur, ptcur)
                        P.cp(pw[:, (lev % 2) * 256:(lev % 2) * 256 + 256],
                             pA[:, 0:256], eng=ce)
                    else:
                        P.cp(pw[:, (lev % 2) * 256:(lev % 2) * 256 + 128],
                             pA[:, 0:128], eng=ce)
                    pcur = pw[:, (lev % 2) * 256:(lev % 2) * 256 + 128]
                    ptcur = pw[:, (lev % 2) * 256 + 128:(lev % 2) * 256 + 256]
                    P.mm(pA[:, 256:384], pcur, Eb[:, ecur * 128:(ecur + 1) * 128],
                         start=True, stop=False)
                    P.mm(pA[:, 256:384], c_idb[:, :], Eb[:, ecur * 128:(ecur + 1) * 128],
                         start=False, stop=True)
                    ecur ^= 1
                    P.cp(Eb[:, ecur * 128:(ecur + 1) * 128], pA[:, 256:384], eng=ce)
                invT = Eb[:, ecur * 128:(ecur + 1) * 128]
                P.tr(p_t[:, 768:896], vf(i), c_idb[:, :])
                P.ts(bvb, p_t[:, 768:896], colc[:, 80 + i:81 + i],
                     None, ALU.mult)
                P.mm(pA[:, 0:128], invT, bvb)
                P.cp(ub, pA[:, 0:128], eng=ce)
                P.ts(ksb, ktk, ecol[:, 0 + i:1 + i], None, ALU.mult)
                P.mm(pA[:, 128:256], ksb, invT)
                P.cp(wtb, pA[:, 128:256], eng=ce)
                P.ts(ybb, ktk, ecol[:, 32 + i:33 + i], None, ALU.mult)
                P.cp(Sbb, S1[:, i * 128:(i + 1) * 128])
                P.mm(pA[:, 256:384], wtb, Sbb)
                P.tt(dbb, ub, pA[:, 256:384], ALU.subtract)
                if own:
                    P.mm(pA[:, 0:128], qf(j), Sbb)
                    P.ts(ob[:, i * 128:(i + 1) * 128], pA[:, 0:128],
                         ecol[:, 16 + i:17 + i], 1.0 / float(np.sqrt(128.0)),
                         ALU.mult, ALU.mult)
                    P.mm(pA[:, 128:256], MTmat, dbb)
                    P.tt(ob[:, i * 128:(i + 1) * 128], ob[:, i * 128:(i + 1) * 128],
                         pA[:, 128:256], ALU.add)
                P.mm(pA[:, 384:512], ybb, dbb)
                P.stt(S2[:, i * 128:(i + 1) * 128], S1[:, i * 128:(i + 1) * 128],
                      ecol[:, 48 + i:49 + i], pA[:, 384:512], ALU.mult, ALU.add)
            S1, S2 = S2, S1
            if own:
                oc = ch - CH_OWN
                P.op("scalar", "activation", osq[:, :], ob[:, :], AF.Square)
                P.red(rsd[:, 0:16], osq[:, :].rearrange("p (h v) -> p h v", v=128),
                      ALU.add)
                P.act(rsd[:, 0:16], rsd[:, 0:16], AF.Ln, bias=1e-6, scale=1.0 / 128)
                P.act(rsd[:, 0:16], rsd[:, 0:16], AF.Exp, scale=-0.5)
                P.tt(osq[:, :], ob[:, :], zsn[:, lc * D:(lc + 1) * D], ALU.mult)
                for h in range(HV):
                    P.ts(o2[:, h * 128:(h + 1) * 128], osq[:, h * 128:(h + 1) * 128],
                         rsd[:, h:h + 1], None, ALU.mult)
                for t in range(0, 16, 8):
                    ptx = [p_t0, p_t1][(t // 8) % 2]
                    for dt2 in range(t, t + 8):
                        P.tr(ptx[:, (dt2 - t) * 128:(dt2 - t + 1) * 128],
                             o2[:, dt2 * 128:(dt2 + 1) * 128], c_idb[:, :])
                    P.cp(ofmb[:, t * 128:(t + 8) * 128], ptx[:, 0:1024])
                P.dma("scalar",
                      ofm_d.ap().rearrange("d p t -> p d t")[:, :, oc * 128:(oc + 1) * 128],
                      ofmb[:, :].rearrange("p (d t) -> p d t", t=128))
                P.fence("scalar")
    P.barrier()
    sB.close()
    sAB.close()

    sB2 = contextlib.ExitStack()
    s2 = lambda n_, s, dt_: sB2.enter_context(nc.sbuf_tensor(n_, s, dt_))
    p2 = lambda n_, s, dt_: sB2.enter_context(nc.psum_tensor(n_, s, dt_))
    wo = s2("wo", [128, DT * D], BF16)
    gw = s2("gw", [128, DT * 16], BF16)
    og = s2("og", [128, DT * 512], BF16)
    xc = s2("xc", [128, D], F32)
    x1b = s2("x1b", [128, D], F32)
    x1c = s2("x1c", [128, D], F32)
    l2b = s2("l2b", [128, D], BF16)
    l2f = s2("l2f", [128, D], BF16)
    sct = s2("sct", [128, 4], F32)
    sc16 = s2("sc16", [128, 16], F32)
    tkA = s2("tkA", [128, 16], F32)
    tkB = s2("tkB", [128, 16], F32)
    tkG = s2("tkG", [128, 4], F32)
    tkW = s2("tkW", [128, 12], F32)
    b2_p = p2("b2_p", [128, 512], F32)
    b2_q = p2("b2_q", [128, 512], F32)
    b2_s = p2("b2_s", [128, 16], F32)
    b2_t = p2("b2_t", [128, 1024], BF16)
    for dt2 in range(DT):
        P.dma("gpsimd", wo[:, dt2 * D:(dt2 + 1) * D], wout[dt2 * 128:(dt2 + 1) * 128, :])
        P.dma("gpsimd", gw[:, dt2 * 16:(dt2 + 1) * 16], gwt[dt2 * 128:(dt2 + 1) * 128, :])
    for grp in range(4):
        g0 = grp * 512
        P.dma("sync", og[:, :].rearrange("p (d t) -> p d t", t=512),
              ofm_d.ap().rearrange("d p t -> p d t")[:, :, g0:g0 + 512])
        for tc in range(4):
            oc = grp * 4 + tc
            P.dma("sync", xc[:, :], xe[2560 + g0 + tc * 128:2560 + g0 + (tc + 1) * 128, :])
            pp = [b2_p, b2_q]
            for nb in range(4):
                ps = pp[nb % 2]
                for dt2 in range(DT):
                    P.mm(ps[:, :], og[:, dt2 * 512 + tc * 128:dt2 * 512 + tc * 128 + 128],
                         wo[:, dt2 * D + nb * 512:dt2 * D + (nb + 1) * 512],
                         start=(dt2 == 0), stop=(dt2 == DT - 1))
                P.tt(x1b[:, nb * 512:(nb + 1) * 512], xc[:, nb * 512:(nb + 1) * 512],
                     ps[:, :], ALU.add)
            P.dma("scalar", x1_d.ap()[g0 + tc * 128:g0 + (tc + 1) * 128, :], x1b[:, :])
            P.red(sct[:, 0:1], x1b[:, :], ALU.add)
            P.ts(sct[:, 1:2], sct[:, 0:1], 1.0 / D, None, ALU.mult)
            P.ts(x1c[:, :], x1b[:, :], sct[:, 1:2], None, ALU.subtract)
            P.op("scalar", "activation", x1b[:, :], x1c[:, :], AF.Square)
            P.red(sct[:, 2:3], x1b[:, :], ALU.add)
            P.act(sct[:, 3:4], sct[:, 2:3], AF.Ln, bias=1e-5, scale=1.0 / D)
            P.act(sct[:, 3:4], sct[:, 3:4], AF.Exp, scale=-0.5)
            P.ts(l2b[:, :], x1c[:, :], sct[:, 3:4], None, ALU.mult)
            for t in range(0, 16, 8):
                for dt2 in range(t, t + 8):
                    P.tr(b2_t[:, (dt2 - t) * 128:(dt2 - t + 1) * 128],
                         l2b[:, dt2 * 128:(dt2 + 1) * 128], c_idb[:, :])
                P.cp(l2f[:, t * 128:(t + 8) * 128], b2_t[:, 0:1024])
            P.dma("scalar",
                  l2fm_d.ap().rearrange("d p t -> p d t")[:, :, oc * 128:(oc + 1) * 128],
                  l2f[:, :].rearrange("p (d t) -> p d t", t=128))
            P.fence("scalar")
            for dt2 in range(DT):
                P.mm(b2_s[:, :], l2f[:, dt2 * 128:(dt2 + 1) * 128],
                     gw[:, dt2 * 16:(dt2 + 1) * 16],
                     start=(dt2 == 0), stop=(dt2 == DT - 1))
            P.act(sc16[:, :], b2_s[:, :], AF.Exp, scale=-1.0)
            P.ts(sc16[:, :], sc16[:, :], 1.0, None, ALU.add)
            P.op("vector", "reciprocal", sc16[:, :], sc16[:, :])
            if oc == 15:
                P.dma("sync", dbg_d.ap()[:, 0:16], sc16[:, :])
            i16 = c_iota[:, 0:16]
            i16p = c_iota[:, 16:32]
            i4 = c_iota[:, 32:36]
            i4p = c_iota[:, 36:40]
            P.red(tkG[:, 0:4], sc16[:, :].rearrange("p (g e) -> p g e", e=4), ALU.max)
            P.red(tkW[:, 0:1], tkG[:, 0:4], ALU.max)
            P.ts(tkB[:, 0:4], tkG[:, 0:4], tkW[:, 0:1], None, ALU.is_equal)
            P.stt(tkB[:, 4:8], tkB[:, 0:4], -65536.0, i4p, ALU.mult, ALU.add)
            P.red(tkW[:, 1:2], tkB[:, 4:8], ALU.min)
            P.ts(tkB[:, 0:4], i4, tkW[:, 1:2], None, ALU.is_equal)
            P.stt(tkG[:, 0:4], tkB[:, 0:4], -1e9, tkG[:, 0:4], ALU.mult, ALU.add)
            P.red(tkW[:, 2:3], tkG[:, 0:4], ALU.max)
            P.ts(tkB[:, 4:8], tkG[:, 0:4], tkW[:, 2:3], None, ALU.is_equal)
            P.stt(tkB[:, 8:12], tkB[:, 4:8], -65536.0, i4p, ALU.mult, ALU.add)
            P.red(tkW[:, 3:4], tkB[:, 8:12], ALU.min)
            P.ts(tkB[:, 4:8], i4, tkW[:, 3:4], None, ALU.is_equal)
            P.tt(tkG[:, 0:4], tkB[:, 0:4], tkB[:, 4:8], ALU.add)
            for g in range(4):
                P.ts(tkA[:, g * 4:(g + 1) * 4], sc16[:, g * 4:(g + 1) * 4],
                     tkG[:, g:g + 1], None, ALU.mult)
            for k4 in range(4):
                P.red(tkW[:, 4 + k4:5 + k4], tkA[:, :], ALU.max)
                P.ts(tkB[:, 0:16], tkA[:, :], tkW[:, 4 + k4:5 + k4], None,
                     ALU.is_equal)
                P.stt(tkB[:, 0:16], tkB[:, 0:16], -65536.0, i16p, ALU.mult, ALU.add)
                P.red(tkW[:, 8 + k4:9 + k4], tkB[:, 0:16], ALU.min)
                P.ts(tkB[:, 0:16], i16, tkW[:, 8 + k4:9 + k4], None, ALU.is_equal)
                P.stt(tkA[:, :], tkB[:, 0:16], -1e9, tkA[:, :], ALU.mult, ALU.add)
            P.red(tkW[:, 0:1], tkW[:, 4:8], ALU.add)
            P.op("vector", "reciprocal", tkW[:, 1:2], tkW[:, 0:1])
            if oc == 15:
                P.dma("sync", dbg_d.ap()[:, 16:28], tkW[:, :])
                P.dma("sync", dbg_d.ap()[:, 28:44], tkA[:, :])
                P.dma("sync", dbg_d.ap()[:, 44:48], tkG[:, :])
            cslice = comb[:, oc * 16:(oc + 1) * 16]
            P.op("vector", "memset", cslice, 0.0)
            for k4 in range(4):
                P.ts(tkW[:, 2:3], tkW[:, 4 + k4:5 + k4], tkW[:, 1:2], None, ALU.mult)
                P.ts(tkB[:, 0:16], i16, tkW[:, 8 + k4:9 + k4], None, ALU.is_equal)
                P.stt(cslice, tkB[:, 0:16], tkW[:, 2:3], cslice, ALU.mult, ALU.add)
    P.dma("sync", comb_d.ap(), comb[:, :])
    P.barrier()
    sB2.close()

    sC = contextlib.ExitStack()
    s3 = lambda n_, s, dt_: sC.enter_context(nc.sbuf_tensor(n_, s, dt_))
    p3 = lambda n_, s, dt_: sC.enter_context(nc.psum_tensor(n_, s, dt_))
    yac = s3("yac", [128, 8 * D], F32)
    l2h = s3("l2h", [128, DT * 1024], BF16)
    gbf = s3("gbf", [128, 11 * 1024], BF16)
    w1b = s3("w1b", [128, 2 * DT * 128], BF16)
    w3b = s3("w3b", [128, 2 * DT * 128], BF16)
    w2b = s3("w2b", [128, 11 * D], BF16)
    sil = s3("sil", [128, 1024], BF16)
    c_pa0 = p3("c_pa0", [128, 512], F32)
    c_pa1 = p3("c_pa1", [128, 512], F32)
    c_pb0 = p3("c_pb0", [128, 512], F32)
    c_pb1 = p3("c_pb1", [128, 512], F32)
    c_py = p3("c_py", [128, 512], F32)
    c_pz = p3("c_pz", [128, 512], F32)
    IT = 11

    def c_srcs(e, it):
        if e < 16:
            return (w1t[e * D:(e + 1) * D, it * 128:(it + 1) * 128],
                    w3t[e * D:(e + 1) * D, it * 128:(it + 1) * 128],
                    w2t[e * 1408 + it * 128: e * 1408 + (it + 1) * 128, :])
        sh = e - 16
        return (ws1t[:, sh * 1408 + it * 128: sh * 1408 + (it + 1) * 128],
                ws3t[:, sh * 1408 + it * 128: sh * 1408 + (it + 1) * 128],
                ws2t[sh * 1408 + it * 128: sh * 1408 + (it + 1) * 128, :])

    def c_load13(e, it):
        gi = e * IT + it
        buf = (gi % 2) * DT * 128
        for m, wb in ((0, w1b), (1, w3b)):
            blk0 = (e * 2 + m) * DT
            P.dma("sync", wb[:, buf:buf + DT * 128].rearrange("p (d f) -> p d f", f=128),
                  wffd13.ap().rearrange("(b p) f -> p b f", p=128)[:, blk0:blk0 + DT, it * 128:(it + 1) * 128])
        P.fence("sync")

    def c_loadw2(e, it):
        gi = e * IT + it
        P.dma("scalar", w2b[:, it * D:(it + 1) * D],
              wffd2.ap()[gi * 128:(gi + 1) * 128, :])
        P.fence("scalar")

    for half in range(2):
        t0 = half * 1024
        P.dma("sync", yac[:, :].rearrange("p (c d) -> p c d", d=D),
              x1_d.ap().rearrange("(c p) d -> p c d", p=128)[:, half * 8:half * 8 + 8, :])
        P.dma("sync", l2h[:, :].rearrange("p (d t) -> p d t", t=1024),
              l2fm_d.ap().rearrange("d p t -> p d t")[:, :, t0:t0 + 1024])
        P.fence("sync")
        prep_drain(10 ** 6)
        c_load13(0, 0)
        for e in range(18):
            for it in range(IT):
                git = e * IT + it
                buf = (git % 2) * DT * 128
                if it == 0:
                    c_loadw2(e, 0)
                if it + 1 < IT:
                    c_load13(e, it + 1)
                    c_loadw2(e, it + 1)
                elif e + 1 < 18:
                    c_load13(e + 1, 0)
                for nt in range(2):
                    qa = (git * 2 + nt) % 2
                    pa = [c_pa0, c_pa1][qa]
                    pb = [c_pb0, c_pb1][qa]
                    for kt in range(DT):
                        P.mm(pa[:, :],
                             w1b[:, buf + kt * 128:buf + (kt + 1) * 128],
                             l2h[:, kt * 1024 + nt * 512:kt * 1024 + (nt + 1) * 512],
                             start=(kt == 0), stop=(kt == DT - 1))
                    for kt in range(DT):
                        P.mm(pb[:, :],
                             w3b[:, buf + kt * 128:buf + (kt + 1) * 128],
                             l2h[:, kt * 1024 + nt * 512:kt * 1024 + (nt + 1) * 512],
                             start=(kt == 0), stop=(kt == DT - 1))
                    P.act(sil[:, qa * 512:(qa + 1) * 512], pa[:, :], AF.Silu)
                    P.tt(gbf[:, it * 1024 + nt * 512:it * 1024 + (nt + 1) * 512],
                         sil[:, qa * 512:(qa + 1) * 512], pb[:, :], ALU.mult)
            pp = [c_py, c_pz]
            for tq in range(8):
                oc = half * 8 + tq
                for nb in range(4):
                    ps = pp[nb % 2]
                    for it in range(IT):
                        P.mm(ps[:, :], gbf[:, it * 1024 + tq * 128:it * 1024 + tq * 128 + 128],
                             w2b[:, it * D + nb * 512:it * D + (nb + 1) * 512],
                             start=(it == 0), stop=(it == IT - 1))
                    ysl = yac[:, tq * D + nb * 512:tq * D + (nb + 1) * 512]
                    if e < 16:
                        P.stt(ysl, ps[:, :], comb[:, oc * 16 + e:oc * 16 + e + 1],
                              ysl, ALU.mult, ALU.add)
                    else:
                        P.stt(ysl, ps[:, :], 1.0, ysl, ALU.mult, ALU.add)
        P.dma("scalar",
              yout[half * 1024:(half + 1) * 1024, :].rearrange("(c p) d -> p c d", p=128),
              yac[:, :].rearrange("p (c d) -> p c d", d=D))
        P.fence("scalar")
    P.barrier()
    sC.close()

    for name in sorted(TAPS):
        t = scratch[name]
        shp = list(t.shape)
        n0 = int(np.prod(shp[:-1]))
        flat_in = t.ap()
        if len(shp) == 3:
            flat_in = flat_in.rearrange("a b c -> (a b) c")
        o = nc.dram_tensor("tap_" + name, [n0, shp[-1]], t.dtype,
                           kind="ExternalOutput").ap()
        with nc.sbuf_tensor("tapb_" + name, [128, shp[-1]], t.dtype) as tb:
            for r in range(0, n0, 128):
                P.dma("sync", tb[:, :], flat_in[r:r + 128, :])
                P.dma("sync", o[r:r + 128, :], tb[:, :])

    with nc.Block() as blk:
        P.emit(nc, blk, outer)
    outer.close()
    return nc



_CACHE = {}


def _consts():
    idb = np.eye(128, dtype=BF)
    idf = np.eye(128, dtype=np.float32)
    sel = np.zeros((16, 2048), np.float32)
    for i in range(16):
        sel[i, i * 128:(i + 1) * 128] = 1.0
    onek = np.zeros((128, 128), np.float32)
    for j in range(8):
        onek[:, j * 16 + 2 * j] = 1.0
        onek[:, j * 16 + 2 * j + 1] = 1.0
    iotas = np.zeros((128, 40), np.float32)
    iotas[:, 0:16] = np.arange(16)[None, :]
    iotas[:, 16:32] = np.arange(16)[None, :] + 65536.0
    iotas[:, 32:36] = np.arange(4)[None, :]
    iotas[:, 36:40] = np.arange(4)[None, :] + 65536.0
    masks = np.zeros((128, 256), np.float32)
    masks[:, 0:128] = np.tril(np.ones((128, 128), np.float32), -1)
    masks[:, 128:256] = np.triu(np.ones((128, 128), np.float32), 0)
    return dict(idb=idb, idf=idf, sel16=sel, onek=onek.astype(BF),
                iotas=iotas, masks=masks.astype(BF))


def kernel(**inputs):
    x = np.asarray(inputs["L0_attn_hidden_states"])
    Wq = np.asarray(inputs["L0_attn_W_qkvz"])
    Wba = np.asarray(inputs["L0_attn_W_ba"])
    conv_w = np.asarray(inputs["L0_attn_conv_w"])
    dt_bias = np.asarray(inputs["L0_attn_dt_bias"])
    A_log = np.asarray(inputs["L0_attn_A_log"])
    nw = np.asarray(inputs["L0_attn_norm_weight"])
    Wo = np.asarray(inputs["L0_attn_W_out"])
    gate_w = np.asarray(inputs["L0_ffn_gate_w"])
    W1 = np.asarray(inputs["L0_ffn_W1"])
    W2 = np.asarray(inputs["L0_ffn_W2"])
    W3 = np.asarray(inputs["L0_ffn_W3"])
    Ws1 = np.asarray(inputs["L0_ffn_Ws1"])
    Ws2 = np.asarray(inputs["L0_ffn_Ws2"])
    Ws3 = np.asarray(inputs["L0_ffn_Ws3"])

    qc = np.concatenate([np.arange(j * 768, j * 768 + 128) for j in range(8)])
    kc = np.concatenate([np.arange(j * 768 + 128, j * 768 + 256) for j in range(8)])
    vc = np.concatenate([np.arange(j * 768 + 256, j * 768 + 512) for j in range(8)])
    zc = np.concatenate([np.arange(j * 768 + 512, j * 768 + 768) for j in range(8)])
    bc = np.concatenate([np.arange(j * 4, j * 4 + 2) for j in range(8)])
    ac = np.concatenate([np.arange(j * 4 + 2, j * 4 + 4) for j in range(8)])
    wcat = np.ascontiguousarray(
        np.concatenate([Wq[:, qc], Wq[:, kc], Wq[:, vc], Wq[:, zc],
                        Wba[:, bc], Wba[:, ac]], axis=1))
    convd = np.zeros((4, 32, 128, 128), np.float32)
    idx = np.arange(128)
    for j in range(4):
        for ct in range(32):
            convd[j, ct, idx, idx] = conv_w[ct * 128 + idx, j]
    consts = _consts()
    shared = dict(
        wcat=wcat,
        convd=np.ascontiguousarray(convd.transpose(2, 0, 1, 3).reshape(128, 4 * 32 * 128)).astype(BF),
        alog=A_log.reshape(16, 1).astype(np.float32),
        dtb=dt_bias.reshape(16, 1).astype(np.float32),
        nwbi=np.broadcast_to(np.tile(nw, 16), (128, 2048)).astype(BF).copy(),
        wout=Wo,
        gwt=np.ascontiguousarray(gate_w.T),
        w1t=np.ascontiguousarray(W1.transpose(0, 2, 1)).reshape(16 * 2048, 1408),
        w3t=np.ascontiguousarray(W3.transpose(0, 2, 1)).reshape(16 * 2048, 1408),
        w2t=np.ascontiguousarray(W2.transpose(0, 2, 1)).reshape(16 * 1408, 2048),
        ws1t=np.ascontiguousarray(Ws1.T),
        ws3t=np.ascontiguousarray(Ws3.T),
        ws2t=np.ascontiguousarray(Ws2.T),
        **consts,
    )
    in_maps = []
    for c in range(8):
        b, h = c // 2, c % 2
        xeb = np.zeros((EXT, D), np.float32)
        m = np.zeros(EXT, np.float32)
        if h == 1:
            xeb[512:2560] = x[b, 0:2048]
            m[512:2560] = 1.0
        xeb[2560:] = x[b, h * 2048:(h + 1) * 2048]
        m[2560:] = 1.0
        mhb = np.broadcast_to(m[None, :], (16, EXT)).copy()
        in_maps.append(dict(shared, xe=xeb, mh=mhb))

    if "nc" not in _CACHE:
        _CACHE["nc"] = build_nc()
    nc = _CACHE["nc"]
    res = run_bass_kernel_spmd(nc, in_maps, core_ids=list(range(8)))
    out = np.zeros((4, 4096, 2048), np.float32)
    for c in range(8):
        b, h = c // 2, c % 2
        out[b, h * 2048:(h + 1) * 2048] = res.results[c]["yout"]
    return out


# version 23
# speedup vs baseline: 1.1687x; 1.1687x over previous
import contextlib
import numpy as np
import ml_dtypes
import concourse.bass as bass
import concourse.mybir as mybir
from concourse.bass_utils import run_bass_kernel_spmd

F32 = mybir.dt.float32
BF16 = mybir.dt.bfloat16
AF = mybir.ActivationFunctionType
ALU = mybir.AluOpType
AX = mybir.AxisListType
BF = ml_dtypes.bfloat16

EXT = 4608
NCH = 36
CH_OWN = 20
D = 2048
DT = 16
HV, HK = 16, 8
DK = DV = 128
NGRP = 9
LN128 = float(np.log(128.0))

TAPS = set()


def _is_ap(x):
    return hasattr(x, "tensor") and hasattr(x, "ap") and hasattr(x, "offset")


def _rng(apx):
    t = apx.tensor
    name = t.name
    if "PSum" in type(t).__name__:
        return (name, 0, 1 << 60)
    off = int(apx.offset)
    lo = hi = off
    for pair in apx.ap:
        st, cnt = int(pair[0]), int(pair[1])
        if cnt > 1:
            d = st * (cnt - 1)
            if d >= 0:
                hi += d
            else:
                lo += d
    return (name, lo, hi + 1)


def _fps(fn, a, k):
    W, R = [], []
    def add(lst, x):
        if _is_ap(x):
            lst.append(x)
    if fn == "dma_start":
        add(W, a[0]); add(R, a[1])
    elif fn == "matmul":
        add(W, a[0]); add(R, a[0]); add(R, a[1]); add(R, a[2])
    elif fn == "transpose":
        add(W, a[0]); add(R, a[1]); add(R, a[2])
    elif fn in ("tensor_copy", "reciprocal", "tensor_reduce", "activation"):
        add(W, a[0]); add(R, a[1])
        for key in ("bias", "scale"):
            if key in k:
                add(R, k[key])
    elif fn in ("tensor_tensor", "tensor_tensor_scan"):
        add(W, a[0]); add(R, a[1]); add(R, a[2])
        if fn == "tensor_tensor_scan" and len(a) > 3:
            add(R, a[3])
    elif fn in ("tensor_scalar", "scalar_tensor_tensor"):
        add(W, a[0]); add(R, a[1]); add(R, a[2])
        if len(a) > 3:
            add(R, a[3])
    elif fn in ("memset", "iota"):
        add(W, a[0])
    else:
        raise AssertionError("footprint: unknown fn " + fn)
    return W, R


_STREAM_OF = {("tensor", False): "T", ("vector", False): "V",
              ("scalar", False): "A", ("gpsimd", False): "G",
              ("sync", True): "SD", ("gpsimd", True): "GD",
              ("scalar", True): "XD", ("vector", True): "VD"}


class Prog:

    EPS_C = 30000
    EPS_D = 3000

    def __init__(self):
        self.ops = []
        self.cbias = None
        self.bias_cols = {}

    def op(self, eng, fn, *a, **k):
        self.ops.append((eng, fn, a, k, False))

    def dma(self, eng, out, in_, **k):
        self.ops.append((eng, "dma_start", (out, in_), k, True))

    def fence(self, eng):
        self.ops.append((eng, "__fence__", (), {}, True))

    def barrier(self):
        self.ops.append(("sync", "__barrier__", (), {}, False))

    def mm(self, out, lhsT, rhs, start=True, stop=True):
        self.op("tensor", "matmul", out, lhsT, rhs, start=start, stop=stop)

    def tr(self, out, in_, ident):
        self.op("tensor", "transpose", out, in_, ident)

    def cp(self, out, in_, eng="vector"):
        self.op(eng, "tensor_copy", out, in_)

    def tt(self, out, a, b, alu, eng="vector"):
        self.op(eng, "tensor_tensor", out, a, b, alu)

    def ts(self, out, in_, s1, s2, op0, op1=ALU.bypass, eng="vector"):
        self.op(eng, "tensor_scalar", out, in_, s1, s2, op0, op1)

    def stt(self, out, in0, scalar, in1, op0, op1, eng="vector"):
        self.op(eng, "scalar_tensor_tensor", out, in0, scalar, in1, op0, op1)

    def act(self, out, in_, fn, bias=0.0, scale=1.0):
        if isinstance(bias, float) and bias != 0.0 and self.cbias is not None:
            if bias not in self.bias_cols:
                col = len(self.bias_cols)
                self.bias_cols[bias] = col
                self.op("vector", "memset", self.cbias[:, col:col + 1], bias)
            col = self.bias_cols[bias]
            rows = out.shape[0]
            bias = self.cbias[0:rows, col:col + 1]
        self.op("scalar", "activation", out, in_, fn, bias=bias, scale=scale)

    def red(self, out, in_, alu, axis=AX.X):
        self.op("vector", "tensor_reduce", out, in_, axis, alu)


    def _schedule(self):
        ops = self.ops
        cnt = {s: 0 for s in set(_STREAM_OF.values())}
        qbound = {s: 0 for s in cnt}
        pending_gate = {s: None for s in cnt}
        wm = {}
        log = {}
        sched = []
        dma_streams = {"SD", "GD", "XD", "VD"}

        floor = {s: 0 for s in cnt}

        def close(q, c=None):
            v = cnt[q] if c is None else c
            if v > qbound[q]:
                qbound[q] = v
                pending_gate[q] = v

        for i, (en, fn, a, k, isdma) in enumerate(ops):
            if fn == "__fence__":
                close(_STREAM_OF[(en, True)])
                sched.append(None)
                continue
            if fn == "__barrier__":
                for s in cnt:
                    floor[s] = cnt[s]
                    if s in dma_streams:
                        close(s)
                sched.append(None)
                continue
            st = _STREAM_OF[(en, isdma)]
            W, R = _fps(fn, a, k)
            deps = {s: f for s, f in floor.items() if f > 0}
            entries = []
            for x in W:
                entries.append((_rng(x), True))
            for x in R:
                entries.append((_rng(x), False))
            for (name, lo, hi), iw in entries:
                ll = log.get(name)
                if not ll:
                    continue
                for (s2, c2, lo2, hi2, w2) in ll:
                    if (iw or w2) and lo < hi2 and lo2 < hi:
                        if deps.get(s2, 0) < c2:
                            deps[s2] = c2
            waits = []
            for s2, c2 in deps.items():
                if s2 == st and st == "T":
                    continue
                if s2 in dma_streams:
                    if c2 > qbound[s2]:
                        close(s2)
                    req = qbound[s2]
                else:
                    req = c2
                key = (st, s2)
                if wm.get(key, 0) < req:
                    wm[key] = req
                    waits.append((s2, req))
            gate = None
            if isdma:
                q = cnt[st]
                if q > 0 and q % Prog.EPS_D == 0:
                    close(st)
                if pending_gate[st] is not None:
                    gate = pending_gate[st]
                    pending_gate[st] = None
                    key = (st, st)
                    if wm.get(key, 0) >= gate:
                        gate = None
                    else:
                        wm[key] = gate
            cnt[st] += 1
            c = cnt[st]
            for (name, lo, hi), iw in entries:
                ll = log.setdefault(name, [])
                ll.append((st, c, lo, hi, iw))
                if len(ll) > 800:
                    best = {}
                    for (s2, c2, _, _, w2) in ll:
                        if best.get(s2, 0) < c2:
                            best[s2] = c2
                    log[name] = [(s2, c2, 0, 1 << 60, True)
                                 for s2, c2 in best.items()]
            sched.append((st, c, waits, gate))
        return sched, cnt

    def emit(self, nc, blk, stack):
        sched, totals = self._schedule()
        dma_streams = {"SD", "GD", "XD", "VD"}
        sems = {}
        for s, tot in totals.items():
            if tot == 0:
                continue
            eps = Prog.EPS_D if s in dma_streams else Prog.EPS_C
            neps = (tot + eps - 1) // eps
            sems[s] = [stack.enter_context(nc.semaphore(f"q_{s}{e}"))
                       for e in range(neps)]

        def sem_val(s, c):
            eps = Prog.EPS_D if s in dma_streams else Prog.EPS_C
            e = (c - 1) // eps
            v = c - e * eps
            if s in dma_streams:
                v *= 16
            return sems[s][e], v

        ops = self.ops
        eng_items = {}
        for i, p in enumerate(ops):
            if p[1] in ("__fence__", "__barrier__"):
                continue
            eng_items.setdefault(p[0], []).append(i)

        def runner(idxs):
            def run(eng):
                for i in idxs:
                    en, fn, a, k, isdma = ops[i]
                    st, c, waits, gate = sched[i]
                    if gate is not None:
                        sm, v = sem_val(st, gate)
                        eng.wait_ge(sm, v)
                    for s2, c2 in waits:
                        sm, v = sem_val(s2, c2)
                        eng.wait_ge(sm, v)
                    inst = getattr(eng, fn)(*a, **k)
                    sm, v = sem_val(st, c)
                    inst.then_inc(sm, 16 if isdma else 1)
                for s, tot in totals.items():
                    if tot:
                        sm, v = sem_val(s, tot)
                        eng.wait_ge(sm, v)
            return run

        def idle(eng):
            for s, tot in totals.items():
                if tot:
                    sm, v = sem_val(s, tot)
                    eng.wait_ge(sm, v)

        for name in ["sync", "scalar", "vector", "tensor", "gpsimd"]:
            idxs = eng_items.get(name, [])
            blk_fn = getattr(blk, name)
            if idxs:
                blk_fn(runner(idxs))
            else:
                blk_fn(idle)


def build_nc():
    nc = bass.Bass("TRN2", target_bir_lowering=False, debug=False)
    ap = lambda t: t.ap()

    xe = ap(nc.dram_tensor("xe", [EXT, D], F32, kind="ExternalInput"))
    mh = ap(nc.dram_tensor("mh", [16, EXT], F32, kind="ExternalInput"))
    wcat = ap(nc.dram_tensor("wcat", [D, 6176], F32, kind="ExternalInput"))
    convd = ap(nc.dram_tensor("convd", [128, 4 * 32 * 128], BF16, kind="ExternalInput"))
    alog = ap(nc.dram_tensor("alog", [16, 1], F32, kind="ExternalInput"))
    dtb = ap(nc.dram_tensor("dtb", [16, 1], F32, kind="ExternalInput"))
    nwbi = ap(nc.dram_tensor("nwbi", [128, D], BF16, kind="ExternalInput"))
    wout = ap(nc.dram_tensor("wout", [D, D], F32, kind="ExternalInput"))
    gwt = ap(nc.dram_tensor("gwt", [D, 16], F32, kind="ExternalInput"))
    w1t = ap(nc.dram_tensor("w1t", [16 * D, 1408], F32, kind="ExternalInput"))
    w3t = ap(nc.dram_tensor("w3t", [16 * D, 1408], F32, kind="ExternalInput"))
    w2t = ap(nc.dram_tensor("w2t", [16 * 1408, D], F32, kind="ExternalInput"))
    ws1t = ap(nc.dram_tensor("ws1t", [D, 2816], F32, kind="ExternalInput"))
    ws3t = ap(nc.dram_tensor("ws3t", [D, 2816], F32, kind="ExternalInput"))
    ws2t = ap(nc.dram_tensor("ws2t", [2816, D], F32, kind="ExternalInput"))
    idb = ap(nc.dram_tensor("idb", [128, 128], BF16, kind="ExternalInput"))
    idf = ap(nc.dram_tensor("idf", [128, 128], F32, kind="ExternalInput"))
    sel16 = ap(nc.dram_tensor("sel16", [16, 2048], F32, kind="ExternalInput"))
    onek = ap(nc.dram_tensor("onek", [128, 128], BF16, kind="ExternalInput"))
    iotas = ap(nc.dram_tensor("iotas", [128, 40], F32, kind="ExternalInput"))
    masks = ap(nc.dram_tensor("masks", [128, 256], BF16, kind="ExternalInput"))

    yout = ap(nc.dram_tensor("yout", [D, D], F32, kind="ExternalOutput"))

    lnfm_d = nc.dram_tensor("lnfm_d", [DT, 128, EXT], BF16, kind="Internal")
    qkv_d = nc.dram_tensor("qkv_d", [32, 128, EXT], BF16, kind="Internal")
    z_d = nc.dram_tensor("z_d", [D, D], BF16, kind="Internal")
    ofm_d = nc.dram_tensor("ofm_d", [DT, 128, D], BF16, kind="Internal")
    x1_d = nc.dram_tensor("x1_d", [D, D], F32, kind="Internal")
    l2fm_d = nc.dram_tensor("l2fm_d", [DT, 128, D], BF16, kind="Internal")
    comb_d = nc.dram_tensor("comb_d", [128, 256], F32, kind="Internal")
    dbg_d = nc.dram_tensor("dbg_d", [128, 64], F32, kind="Internal")
    scratch = dict(lnfm_d=lnfm_d, qkv_d=qkv_d, z_d=z_d, ofm_d=ofm_d,
                   x1_d=x1_d, l2fm_d=l2fm_d, comb_d=comb_d, dbg_d=dbg_d)

    P = Prog()
    outer = contextlib.ExitStack()
    sbo = lambda n_, s, dt_: outer.enter_context(nc.sbuf_tensor(n_, s, dt_))
    comb = sbo("comb", [128, 16 * 16], F32)
    c_idb = sbo("c_idb", [128, 128], BF16)
    c_idf = sbo("c_idf", [128, 128], F32)
    c_iota = sbo("c_iota", [128, 40], F32)
    P.cbias = sbo("cbias", [128, 8], F32)
    P.dma("sync", c_idb[:, :], idb)
    P.dma("sync", c_idf[:, :], idf)
    P.dma("sync", c_iota[:, :], iotas)

    sAB = contextlib.ExitStack()
    sab = lambda n_, s, dt_: sAB.enter_context(nc.sbuf_tensor(n_, s, dt_))
    gml = sab("gml", [16, EXT], F32)[:, :]
    lnbm = sab("lnbm", [16, EXT], F32)[:, :]
    betm = sab("betm", [16, EXT], F32)[:, :]
    sBA = contextlib.ExitStack()
    bab = sBA.enter_context(nc.sbuf_tensor("bab", [16, EXT], F32))[:, :]
    baa = sBA.enter_context(nc.sbuf_tensor("baa", [16, EXT], F32))[:, :]

    sA = contextlib.ExitStack()
    sa = lambda n_, s, dt_: sA.enter_context(nc.sbuf_tensor(n_, s, dt_))
    a_x = sa("a_x", [128, D], F32)
    a_xc = sa("a_xc", [128, D], F32)
    a_sc = sa("a_sc", [128, 4], F32)
    a_ln = sa("a_ln", [128, D], BF16)
    a_fm = sa("a_fm", [128, D], BF16)
    a_pt = sA.enter_context(nc.psum_tensor("a_pt", [128, 1024], BF16))
    for ch in range(NCH):
        P.dma("sync", a_x[:, :], xe[ch * 128:(ch + 1) * 128, :])
        P.red(a_sc[:, 0:1], a_x[:, :], ALU.add)
        P.ts(a_sc[:, 1:2], a_sc[:, 0:1], 1.0 / D, None, ALU.mult)
        P.ts(a_xc[:, :], a_x[:, :], a_sc[:, 1:2], None, ALU.subtract)
        P.op("scalar", "activation", a_x[:, :], a_xc[:, :], AF.Square)
        P.red(a_sc[:, 2:3], a_x[:, :], ALU.add)
        P.act(a_sc[:, 3:4], a_sc[:, 2:3], AF.Ln, bias=1e-5, scale=1.0 / D)
        P.act(a_sc[:, 3:4], a_sc[:, 3:4], AF.Exp, scale=-0.5)
        P.ts(a_ln[:, :], a_xc[:, :], a_sc[:, 3:4], None, ALU.mult)
        for t in range(0, 16, 8):
            for dt2 in range(t, t + 8):
                P.tr(a_pt[:, (dt2 - t) * 128:(dt2 - t + 1) * 128],
                     a_ln[:, dt2 * 128:(dt2 + 1) * 128], c_idb[:, :])
            P.cp(a_fm[:, t * 128:(t + 8) * 128], a_pt[:, :])
        P.dma("sync",
              lnfm_d.ap().rearrange("d p t -> p d t")[:, :, ch * 128:(ch + 1) * 128],
              a_fm[:, :].rearrange("p (d t) -> p d t", t=128))
    P.barrier()
    sA.close()

    QCOL = [0, 1536, 3072, 4608, 6176]
    sA1 = contextlib.ExitStack()
    sa1 = lambda n_, s, dt_: sA1.enter_context(nc.sbuf_tensor(n_, s, dt_))
    pa1 = lambda n_, s, dt_: sA1.enter_context(nc.psum_tensor(n_, s, dt_))
    a_w = sa1("a_w", [128, DT * 1568], BF16)
    a_lg = sa1("a_lg", [128, DT * 512], BF16)
    a_ob = sa1("a_ob", [128, 12 * 512], BF16)
    a_zb = sa1("a_zb", [128, 512], BF16)
    a_p0 = pa1("a_p0", [128, 512], F32)
    a_p1 = pa1("a_p1", [128, 512], F32)
    a_pb = pa1("a_pb", [16, 512], F32)
    for q in range(4):
        c0, c1 = QCOL[q], QCOL[q + 1]
        W = c1 - c0
        for dt2 in range(DT):
            P.dma("gpsimd", a_w[:, dt2 * W:dt2 * W + W],
                  wcat[dt2 * 128:(dt2 + 1) * 128, c0:c1])
        for grp in range(NGRP):
            g0 = grp * 512
            ct_lo, ct_hi = c0 // 128, min(c1, 4096) // 128
            cts = list(range(ct_lo, ct_hi))
            do_z = (grp >= 5 and q >= 2)
            do_ba = (q == 3)
            if not cts and not do_z and not do_ba:
                continue
            P.dma("sync", a_lg[:, :].rearrange("p (d t) -> p d t", t=512),
                  lnfm_d.ap().rearrange("d p t -> p d t")[:, :, g0:g0 + 512])
            pp = [a_p0, a_p1]
            for ci, ct in enumerate(cts):
                ps = pp[ci % 2]
                off = ct * 128 - c0
                for dt2 in range(DT):
                    P.mm(ps[:, :], a_w[:, dt2 * W + off:dt2 * W + off + 128],
                         a_lg[:, dt2 * 512:(dt2 + 1) * 512],
                         start=(dt2 == 0), stop=(dt2 == DT - 1))
                P.cp(a_ob[:, ci * 512:(ci + 1) * 512], ps[:, :])
            if cts:
                P.dma("sync",
                      qkv_d.ap().rearrange("c p t -> p c t")[:, cts[0]:cts[0] + len(cts), g0:g0 + 512],
                      a_ob[:, 0:len(cts) * 512].rearrange("p (c t) -> p c t", t=512))
            if do_ba:
                for dt2 in range(DT):
                    P.mm(a_pb[:, :], a_w[:, dt2 * W + 1536:dt2 * W + 1552],
                         a_lg[:, dt2 * 512:(dt2 + 1) * 512],
                         start=(dt2 == 0), stop=(dt2 == DT - 1))
                P.cp(bab[:, g0:g0 + 512], a_pb[:, :])
                for dt2 in range(DT):
                    P.mm(a_pb[:, :], a_w[:, dt2 * W + 1552:dt2 * W + 1568],
                         a_lg[:, dt2 * 512:(dt2 + 1) * 512],
                         start=(dt2 == 0), stop=(dt2 == DT - 1))
                P.cp(baa[:, g0:g0 + 512], a_pb[:, :])
            if do_z:
                nbs = [0] if q == 2 else [1, 2, 3]
                for tc in range(4):
                    for nb in nbs:
                        off = 4096 + nb * 512 - c0
                        ps = pp[nb % 2]
                        for dt2 in range(DT):
                            P.mm(ps[:, :],
                                 a_lg[:, dt2 * 512 + tc * 128:dt2 * 512 + tc * 128 + 128],
                                 a_w[:, dt2 * W + off:dt2 * W + off + 512],
                                 start=(dt2 == 0), stop=(dt2 == DT - 1))
                        P.cp(a_zb[:, :], ps[:, :])
                        trow = g0 + tc * 128 - 2560
                        P.dma("sync",
                              z_d.ap()[trow:trow + 128, nb * 512:(nb + 1) * 512],
                              a_zb[:, :])
    P.barrier()
    sA1.close()

    sBp = contextlib.ExitStack()
    sbp = lambda n_, s, dt_: sBp.enter_context(nc.sbuf_tensor(n_, s, dt_))
    mhs = sbp("mhs", [16, EXT], F32)[:, :]
    mneg = sbp("mneg", [16, EXT], F32)[:, :]
    b_t0 = sbp("b_t0", [16, EXT], F32)[:, :]
    b_t1 = sbp("b_t1", [16, EXT], F32)[:, :]
    expa = sbp("expa", [16, 2], F32)
    P.dma("sync", mhs, mh)
    P.ts(mneg, mhs, 1e4, -1e4, ALU.mult, ALU.add)
    P.act(b_t0, bab, AF.Exp, scale=-1.0)
    P.ts(b_t1, b_t0, 1.0, None, ALU.add)
    P.op("vector", "reciprocal", betm, b_t1)
    P.tt(betm, betm, mhs, ALU.mult)
    P.act(b_t0, bab, AF.Abs)
    P.act(b_t0, b_t0, AF.Exp, scale=-1.0)
    P.act(b_t0, b_t0, AF.Ln, bias=1.0)
    P.ts(b_t1, bab, -1.0, 0.0, ALU.mult, ALU.max)
    P.tt(b_t0, b_t0, b_t1, ALU.add)
    P.tt(b_t0, b_t0, mhs, ALU.mult)
    P.stt(lnbm, b_t0, -1.0, mneg, ALU.mult, ALU.add)
    P.dma("sync", expa[:, 0:1], alog)
    P.act(expa[:, 0:1], expa[:, 0:1], AF.Exp)
    P.dma("sync", expa[:, 1:2], dtb)
    P.ts(b_t0, baa, expa[:, 1:2], None, ALU.add)
    P.act(b_t1, b_t0, AF.Abs)
    P.act(b_t1, b_t1, AF.Exp, scale=-1.0)
    P.act(b_t1, b_t1, AF.Ln, bias=1.0)
    P.ts(b_t0, b_t0, 1.0, 0.0, ALU.mult, ALU.max)
    P.tt(b_t0, b_t0, b_t1, ALU.add)
    P.ts(gml, b_t0, expa[:, 0:1], -1.0, ALU.mult, ALU.mult)
    P.tt(gml, gml, mhs, ALU.mult)
    P.barrier()
    sBp.close()
    sBA.close()

    sB = contextlib.ExitStack()
    sb = lambda n_, s, dt_: sB.enter_context(nc.sbuf_tensor(n_, s, dt_))
    pb = lambda n_, s, dt_: sB.enter_context(nc.psum_tensor(n_, s, dt_))
    c_sel = sb("c_sel", [16, 2048], F32)
    c_onek = sb("c_onek", [128, 128], BF16)
    c_mask = sb("c_mask", [128, 256], BF16)
    c_nwb = sb("c_nwb", [128, D], BF16)
    ones16 = sb("ones16", [16, 128], F32)
    cdg = sb("cdg", [128, 4 * 32 * 128], BF16)
    qg = sb("qg", [128, 32 * 259], BF16)
    cvo = sb("cvo", [128, 32 * 256], BF16)
    zsn = sb("zsn", [128, 2 * D], BF16)
    sqb = sb("sqb", [128, 2048], BF16)
    lnkq = sb("lnkq", [16, 256], F32)
    compb = sb("compb", [16, 8 * 128], F32)
    colc = sb("colc", [128, 96], F32)
    ecol = sb("ecol", [128, 64], F32)
    gkk = sb("gkk", [128, 128], BF16)
    gkq = sb("gkq", [128, 128], BF16)
    ktk = sb("ktk", [128, 128], BF16)
    dam = sb("dam", [128, 256], F32)
    eDb = sb("eDb", [128, 256], BF16)
    Amat = sb("Amat", [128, 128], BF16)
    ATmat = sb("ATmat", [128, 128], BF16)
    MTmat = sb("MTmat", [128, 128], BF16)
    pw = sb("pw", [128, 2 * 256], BF16)
    Eb = sb("Eb", [128, 2 * 128], BF16)
    ub = sb("ub", [128, 128], BF16)
    ksb = sb("ksb", [128, 128], BF16)
    wtb = sb("wtb", [128, 128], BF16)
    ybb = sb("ybb", [128, 128], BF16)
    dbb = sb("dbb", [128, 128], BF16)
    bvb = sb("bvb", [128, 128], BF16)
    Sbb = sb("Sbb", [128, 128], BF16)
    ob = sb("ob", [128, HV * DV], F32)
    osq = sb("osq", [128, HV * DV], F32)
    o2 = sb("o2", [128, HV * DV], BF16)
    ofmb = sb("ofmb", [128, D], BF16)
    rsd = sb("rsd", [128, 16], F32)
    S1 = sb("S1", [128, HV * DV], F32)
    S2 = sb("S2", [128, HV * DV], F32)
    p_c0 = pb("p_c0", [128, 512], F32)
    p_d = pb("p_d", [128, 256], F32)
    p_g = pb("p_g", [128, 256], F32)
    p_w = pb("p_w", [128, 256], F32)
    p_s = pb("p_s", [128, 128], F32)
    p_t = pb("p_t", [128, 1024], BF16)
    p_kq = pb("p_kq", [16, 256], F32)
    p_hw = pb("p_hw", [128, 128], F32)

    def warm():
        P.mm(p_hw[:, :], c_idb[:, :], c_idb[:, :])

    P.dma("sync", c_sel[:, :], sel16)
    P.dma("sync", c_onek[:, :], onek)
    P.dma("sync", c_mask[:, :], masks)
    P.dma("sync", c_nwb[:, :], nwbi)
    P.op("vector", "memset", ones16[:, :], 1.0)
    P.dma("gpsimd", cdg[:, :], convd)
    P.op("vector", "memset", S1[:, :], 0.0)

    for pair in range(2, 18):
        b0 = pair * 256
        cts = list(range(32)) if pair >= 9 else list(range(8, 32))
        P.dma("sync", qg[:, :].rearrange("p (c t) -> p c t", t=259),
              qkv_d.ap().rearrange("c p t -> p c t")[:, :, b0 - 3:b0 + 256])
        for ct in cts:
            for j in range(4):
                P.mm(p_c0[:, 0:256], cdg[:, (j * 32 + ct) * 128:(j * 32 + ct + 1) * 128],
                     qg[:, ct * 259 + j:ct * 259 + j + 256],
                     start=(j == 0), stop=(j == 3))
            P.act(cvo[:, ct * 256:(ct + 1) * 256], p_c0[:, 0:256], AF.Silu)
        if pair >= 10:
            for lc in range(2):
                trow = b0 + lc * 128 - 2560
                P.dma("sync", zsn[:, lc * D:(lc + 1) * D], z_d.ap()[trow:trow + 128, :])
            for lc in range(2):
                P.act(zsn[:, lc * D:(lc + 1) * D], zsn[:, lc * D:(lc + 1) * D], AF.Silu)
                P.tt(zsn[:, lc * D:(lc + 1) * D], zsn[:, lc * D:(lc + 1) * D],
                     c_nwb[:, :], ALU.mult)
        for lc in range(2):
            ch = pair * 2 + lc
            own = ch >= CH_OWN
            t0 = ch * 128
            lo = lc * 128
            qf = lambda j, lo=lo: cvo[:, j * 256 + lo:j * 256 + lo + 128]
            kf = lambda j, lo=lo: cvo[:, (8 + j) * 256 + lo:(8 + j) * 256 + lo + 128]
            vf = lambda i, lo=lo: cvo[:, (16 + i) * 256 + lo:(16 + i) * 256 + lo + 128]
            P.op("scalar", "activation",
                 sqb[:, 0:1024].rearrange("p (c t) -> p c t", t=128),
                 cvo[:, :].rearrange("p (c t) -> p c t", t=256)[:, 8:16, lo:lo + 128],
                 AF.Square)
            for j in range(8):
                P.mm(p_kq[:, 0:128], c_onek[:, j * 16:(j + 1) * 16],
                     sqb[:, j * 128:(j + 1) * 128],
                     start=(j == 0), stop=(j == 7))
            P.act(lnkq[:, 0:128], p_kq[:, 0:128], AF.Ln, bias=1e-6)
            if own:
                P.op("scalar", "activation",
                     sqb[:, 1024:2048].rearrange("p (c t) -> p c t", t=128),
                     cvo[:, :].rearrange("p (c t) -> p c t", t=256)[:, 0:8, lo:lo + 128],
                     AF.Square)
                for j in range(8):
                    P.mm(p_kq[:, 128:256], c_onek[:, j * 16:(j + 1) * 16],
                         sqb[:, 1024 + j * 128:1024 + (j + 1) * 128],
                         start=(j == 0), stop=(j == 7))
                P.act(lnkq[:, 128:256], p_kq[:, 128:256], AF.Ln, bias=1e-6)
            G = compb[:, 7 * 128:8 * 128]
            P.op("vector", "tensor_tensor_scan", G, ones16[:, :],
                 gml[:, t0:t0 + 128], 0.0, ALU.mult, ALU.add)
            R0, R1, R2, R3, R4, Rb, R1M = (compb[:, s * 128:(s + 1) * 128]
                                           for s in range(7))
            P.tt(R0, G, lnbm[:, t0:t0 + 128], ALU.add)
            P.stt(R0, lnkq[:, 0:128], -0.5, R0, ALU.mult, ALU.add)
            P.stt(R1, lnkq[:, 0:128], -0.5, G, ALU.mult, ALU.subtract)
            if own:
                P.stt(R2, lnkq[:, 128:256], -0.5, G, ALU.mult, ALU.add)
            P.ts(R1M, R1, -0.5 * LN128, None, ALU.add)
            P.ts(R3, R1, G[:, 127:128], None, ALU.add)
            P.ts(R4, G, 0.0, G[:, 127:128], ALU.mult, ALU.add)
            P.cp(Rb, betm[:, t0:t0 + 128])
            slots = [0, 2, 3, 4, 6, 5]
            for si, s in enumerate(slots):
                P.tr(p_w[:, si * 16:(si + 1) * 16],
                     compb[:, s * 128:(s + 1) * 128], c_idf[0:16, 0:16])
            P.cp(colc[:, :], p_w[:, 0:96])
            P.act(ecol[:, :], colc[:, 0:64], AF.Exp)
            for i in range(HV):
                j = i // 2
                if i % 2 == 0:
                    P.mm(p_g[:, 0:128], kf(j), kf(j))
                    P.cp(gkk[:, :], p_g[:, 0:128])
                    P.tt(gkk[:, :], gkk[:, :], c_mask[:, 0:128], ALU.mult)
                    if own:
                        P.mm(p_g[:, 128:256], kf(j), qf(j))
                        P.cp(gkq[:, :], p_g[:, 128:256])
                        P.tt(gkq[:, :], gkq[:, :], c_mask[:, 128:256], ALU.mult)
                    P.tr(p_t[:, 0:128], kf(j), c_idb[:, :])
                    P.cp(ktk[:, :], p_t[:, 0:128])
                P.mm(p_d[:, 0:128], c_sel[:, i * 128:(i + 1) * 128],
                     compb[:, 1 * 128:2 * 128])
                if own:
                    P.mm(p_d[:, 128:256], c_sel[:, i * 128:(i + 1) * 128],
                         compb[:, 2 * 128:3 * 128])
                P.ts(dam[:, 0:128], p_d[:, 0:128], colc[:, 0 + i:1 + i],
                     60.0, ALU.add, ALU.min)
                if own:
                    P.ts(dam[:, 128:256], p_d[:, 128:256], colc[:, 64 + i:65 + i],
                         60.0, ALU.add, ALU.min)
                    P.act(eDb[:, :], dam[:, :], AF.Exp)
                    P.tt(MTmat[:, :], eDb[:, 128:256], gkq[:, :], ALU.mult)
                else:
                    P.act(eDb[:, 0:128], dam[:, 0:128], AF.Exp)
                P.tt(Amat[:, :], eDb[:, 0:128], gkk[:, :], ALU.mult)
                P.tr(p_t[:, 128:256], Amat[:, :], c_idb[:, :])
                warm()
                P.cp(ATmat[:, :], p_t[:, 128:256])
                P.stt(Eb[:, 0:128], ATmat[:, :], -1.0, c_idb[:, :],
                      ALU.mult, ALU.add)
                ecur = 0
                pcur, ptcur = Amat[:, :], ATmat[:, :]
                for lev in range(1, 7):
                    P.mm(p_w[:, 0:128], ptcur, pcur)
                    if lev < 6:
                        P.mm(p_w[:, 128:256], pcur, ptcur)
                        P.cp(pw[:, (lev % 2) * 256:(lev % 2) * 256 + 256], p_w[:, :])
                    else:
                        P.cp(pw[:, (lev % 2) * 256:(lev % 2) * 256 + 128],
                             p_w[:, 0:128])
                    pcur = pw[:, (lev % 2) * 256:(lev % 2) * 256 + 128]
                    ptcur = pw[:, (lev % 2) * 256 + 128:(lev % 2) * 256 + 256]
                    P.mm(p_s[:, :], pcur, Eb[:, ecur * 128:(ecur + 1) * 128],
                         start=True, stop=False)
                    P.mm(p_s[:, :], c_idb[:, :], Eb[:, ecur * 128:(ecur + 1) * 128],
                         start=False, stop=True)
                    ecur ^= 1
                    warm()
                    P.cp(Eb[:, ecur * 128:(ecur + 1) * 128], p_s[:, :])
                invT = Eb[:, ecur * 128:(ecur + 1) * 128]
                P.tr(p_t[:, 256:384], vf(i), c_idb[:, :])
                P.ts(bvb[:, :], p_t[:, 256:384], colc[:, 80 + i:81 + i],
                     None, ALU.mult)
                P.mm(p_w[:, 0:128], invT, bvb[:, :])
                P.cp(ub[:, :], p_w[:, 0:128])
                P.ts(ksb[:, :], ktk[:, :], ecol[:, 0 + i:1 + i], None, ALU.mult)
                P.mm(p_w[:, 128:256], ksb[:, :], invT)
                warm()
                P.cp(wtb[:, :], p_w[:, 128:256])
                P.ts(ybb[:, :], ktk[:, :], ecol[:, 32 + i:33 + i], None, ALU.mult)
                P.cp(Sbb[:, :], S1[:, i * 128:(i + 1) * 128])
                warm()
                P.mm(p_w[:, 0:128], wtb[:, :], Sbb[:, :])
                P.tt(dbb[:, :], ub[:, :], p_w[:, 0:128], ALU.subtract)
                if own:
                    P.mm(p_w[:, 128:256], qf(j), Sbb[:, :])
                    P.ts(ob[:, i * 128:(i + 1) * 128], p_w[:, 128:256],
                         ecol[:, 16 + i:17 + i], 1.0 / float(np.sqrt(128.0)),
                         ALU.mult, ALU.mult)
                    P.mm(p_d[:, 0:128], MTmat[:, :], dbb[:, :])
                    P.tt(ob[:, i * 128:(i + 1) * 128], ob[:, i * 128:(i + 1) * 128],
                         p_d[:, 0:128], ALU.add)
                P.mm(p_s[:, :], ybb[:, :], dbb[:, :])
                P.stt(S2[:, i * 128:(i + 1) * 128], S1[:, i * 128:(i + 1) * 128],
                      ecol[:, 48 + i:49 + i], p_s[:, :], ALU.mult, ALU.add)
            S1, S2 = S2, S1
            if own:
                oc = ch - CH_OWN
                P.op("scalar", "activation", osq[:, :], ob[:, :], AF.Square)
                P.red(rsd[:, 0:16], osq[:, :].rearrange("p (h v) -> p h v", v=128),
                      ALU.add)
                P.act(rsd[:, 0:16], rsd[:, 0:16], AF.Ln, bias=1e-6, scale=1.0 / 128)
                P.act(rsd[:, 0:16], rsd[:, 0:16], AF.Exp, scale=-0.5)
                P.tt(osq[:, :], ob[:, :], zsn[:, lc * D:(lc + 1) * D], ALU.mult)
                for h in range(HV):
                    P.ts(o2[:, h * 128:(h + 1) * 128], osq[:, h * 128:(h + 1) * 128],
                         rsd[:, h:h + 1], None, ALU.mult)
                for t in range(0, 16, 8):
                    for dt2 in range(t, t + 8):
                        P.tr(p_t[:, (dt2 - t) * 128:(dt2 - t + 1) * 128],
                             o2[:, dt2 * 128:(dt2 + 1) * 128], c_idb[:, :])
                    P.cp(ofmb[:, t * 128:(t + 8) * 128], p_t[:, 0:1024])
                P.dma("sync",
                      ofm_d.ap().rearrange("d p t -> p d t")[:, :, oc * 128:(oc + 1) * 128],
                      ofmb[:, :].rearrange("p (d t) -> p d t", t=128))
    P.barrier()
    sB.close()
    sAB.close()

    sB2 = contextlib.ExitStack()
    s2 = lambda n_, s, dt_: sB2.enter_context(nc.sbuf_tensor(n_, s, dt_))
    p2 = lambda n_, s, dt_: sB2.enter_context(nc.psum_tensor(n_, s, dt_))
    wo = s2("wo", [128, DT * D], BF16)
    gw = s2("gw", [128, DT * 16], BF16)
    og = s2("og", [128, DT * 512], BF16)
    xc = s2("xc", [128, D], F32)
    x1b = s2("x1b", [128, D], F32)
    x1c = s2("x1c", [128, D], F32)
    l2b = s2("l2b", [128, D], BF16)
    l2f = s2("l2f", [128, D], BF16)
    sct = s2("sct", [128, 4], F32)
    sc16 = s2("sc16", [128, 16], F32)
    tkA = s2("tkA", [128, 16], F32)
    tkB = s2("tkB", [128, 16], F32)
    tkG = s2("tkG", [128, 4], F32)
    tkW = s2("tkW", [128, 12], F32)
    b2_p = p2("b2_p", [128, 512], F32)
    b2_q = p2("b2_q", [128, 512], F32)
    b2_s = p2("b2_s", [128, 16], F32)
    b2_t = p2("b2_t", [128, 1024], BF16)
    for dt2 in range(DT):
        P.dma("gpsimd", wo[:, dt2 * D:(dt2 + 1) * D], wout[dt2 * 128:(dt2 + 1) * 128, :])
        P.dma("gpsimd", gw[:, dt2 * 16:(dt2 + 1) * 16], gwt[dt2 * 128:(dt2 + 1) * 128, :])
    for grp in range(4):
        g0 = grp * 512
        P.dma("sync", og[:, :].rearrange("p (d t) -> p d t", t=512),
              ofm_d.ap().rearrange("d p t -> p d t")[:, :, g0:g0 + 512])
        for tc in range(4):
            oc = grp * 4 + tc
            P.dma("sync", xc[:, :], xe[2560 + g0 + tc * 128:2560 + g0 + (tc + 1) * 128, :])
            pp = [b2_p, b2_q]
            for nb in range(4):
                ps = pp[nb % 2]
                for dt2 in range(DT):
                    P.mm(ps[:, :], og[:, dt2 * 512 + tc * 128:dt2 * 512 + tc * 128 + 128],
                         wo[:, dt2 * D + nb * 512:dt2 * D + (nb + 1) * 512],
                         start=(dt2 == 0), stop=(dt2 == DT - 1))
                P.tt(x1b[:, nb * 512:(nb + 1) * 512], xc[:, nb * 512:(nb + 1) * 512],
                     ps[:, :], ALU.add)
            P.dma("sync", x1_d.ap()[g0 + tc * 128:g0 + (tc + 1) * 128, :], x1b[:, :])
            P.red(sct[:, 0:1], x1b[:, :], ALU.add)
            P.ts(sct[:, 1:2], sct[:, 0:1], 1.0 / D, None, ALU.mult)
            P.ts(x1c[:, :], x1b[:, :], sct[:, 1:2], None, ALU.subtract)
            P.op("scalar", "activation", x1b[:, :], x1c[:, :], AF.Square)
            P.red(sct[:, 2:3], x1b[:, :], ALU.add)
            P.act(sct[:, 3:4], sct[:, 2:3], AF.Ln, bias=1e-5, scale=1.0 / D)
            P.act(sct[:, 3:4], sct[:, 3:4], AF.Exp, scale=-0.5)
            P.ts(l2b[:, :], x1c[:, :], sct[:, 3:4], None, ALU.mult)
            for t in range(0, 16, 8):
                for dt2 in range(t, t + 8):
                    P.tr(b2_t[:, (dt2 - t) * 128:(dt2 - t + 1) * 128],
                         l2b[:, dt2 * 128:(dt2 + 1) * 128], c_idb[:, :])
                P.cp(l2f[:, t * 128:(t + 8) * 128], b2_t[:, 0:1024])
            P.dma("sync",
                  l2fm_d.ap().rearrange("d p t -> p d t")[:, :, oc * 128:(oc + 1) * 128],
                  l2f[:, :].rearrange("p (d t) -> p d t", t=128))
            for dt2 in range(DT):
                P.mm(b2_s[:, :], l2f[:, dt2 * 128:(dt2 + 1) * 128],
                     gw[:, dt2 * 16:(dt2 + 1) * 16],
                     start=(dt2 == 0), stop=(dt2 == DT - 1))
            P.act(sc16[:, :], b2_s[:, :], AF.Exp, scale=-1.0)
            P.ts(sc16[:, :], sc16[:, :], 1.0, None, ALU.add)
            P.op("vector", "reciprocal", sc16[:, :], sc16[:, :])
            if oc == 15:
                P.dma("sync", dbg_d.ap()[:, 0:16], sc16[:, :])
            i16 = c_iota[:, 0:16]
            i16p = c_iota[:, 16:32]
            i4 = c_iota[:, 32:36]
            i4p = c_iota[:, 36:40]
            P.red(tkG[:, 0:4], sc16[:, :].rearrange("p (g e) -> p g e", e=4), ALU.max)
            P.red(tkW[:, 0:1], tkG[:, 0:4], ALU.max)
            P.ts(tkB[:, 0:4], tkG[:, 0:4], tkW[:, 0:1], None, ALU.is_equal)
            P.stt(tkB[:, 4:8], tkB[:, 0:4], -65536.0, i4p, ALU.mult, ALU.add)
            P.red(tkW[:, 1:2], tkB[:, 4:8], ALU.min)
            P.ts(tkB[:, 0:4], i4, tkW[:, 1:2], None, ALU.is_equal)
            P.stt(tkG[:, 0:4], tkB[:, 0:4], -1e9, tkG[:, 0:4], ALU.mult, ALU.add)
            P.red(tkW[:, 2:3], tkG[:, 0:4], ALU.max)
            P.ts(tkB[:, 4:8], tkG[:, 0:4], tkW[:, 2:3], None, ALU.is_equal)
            P.stt(tkB[:, 8:12], tkB[:, 4:8], -65536.0, i4p, ALU.mult, ALU.add)
            P.red(tkW[:, 3:4], tkB[:, 8:12], ALU.min)
            P.ts(tkB[:, 4:8], i4, tkW[:, 3:4], None, ALU.is_equal)
            P.tt(tkG[:, 0:4], tkB[:, 0:4], tkB[:, 4:8], ALU.add)
            for g in range(4):
                P.ts(tkA[:, g * 4:(g + 1) * 4], sc16[:, g * 4:(g + 1) * 4],
                     tkG[:, g:g + 1], None, ALU.mult)
            for k4 in range(4):
                P.red(tkW[:, 4 + k4:5 + k4], tkA[:, :], ALU.max)
                P.ts(tkB[:, 0:16], tkA[:, :], tkW[:, 4 + k4:5 + k4], None,
                     ALU.is_equal)
                P.stt(tkB[:, 0:16], tkB[:, 0:16], -65536.0, i16p, ALU.mult, ALU.add)
                P.red(tkW[:, 8 + k4:9 + k4], tkB[:, 0:16], ALU.min)
                P.ts(tkB[:, 0:16], i16, tkW[:, 8 + k4:9 + k4], None, ALU.is_equal)
                P.stt(tkA[:, :], tkB[:, 0:16], -1e9, tkA[:, :], ALU.mult, ALU.add)
            P.red(tkW[:, 0:1], tkW[:, 4:8], ALU.add)
            P.op("vector", "reciprocal", tkW[:, 1:2], tkW[:, 0:1])
            if oc == 15:
                P.dma("sync", dbg_d.ap()[:, 16:28], tkW[:, :])
                P.dma("sync", dbg_d.ap()[:, 28:44], tkA[:, :])
                P.dma("sync", dbg_d.ap()[:, 44:48], tkG[:, :])
            cslice = comb[:, oc * 16:(oc + 1) * 16]
            P.op("vector", "memset", cslice, 0.0)
            for k4 in range(4):
                P.ts(tkW[:, 2:3], tkW[:, 4 + k4:5 + k4], tkW[:, 1:2], None, ALU.mult)
                P.ts(tkB[:, 0:16], i16, tkW[:, 8 + k4:9 + k4], None, ALU.is_equal)
                P.stt(cslice, tkB[:, 0:16], tkW[:, 2:3], cslice, ALU.mult, ALU.add)
    P.dma("sync", comb_d.ap(), comb[:, :])
    P.barrier()
    sB2.close()

    sC = contextlib.ExitStack()
    s3 = lambda n_, s, dt_: sC.enter_context(nc.sbuf_tensor(n_, s, dt_))
    p3 = lambda n_, s, dt_: sC.enter_context(nc.psum_tensor(n_, s, dt_))
    yac = s3("yac", [128, 8 * D], F32)
    l2h = s3("l2h", [128, DT * 1024], BF16)
    gbf = s3("gbf", [128, 11 * 1024], BF16)
    w1b = s3("w1b", [128, 2 * DT * 128], BF16)
    w3b = s3("w3b", [128, 2 * DT * 128], BF16)
    w2b = s3("w2b", [128, 11 * D], BF16)
    sil = s3("sil", [128, 1024], BF16)
    c_pa = p3("c_pa", [128, 1024], F32)
    c_pb = p3("c_pb", [128, 1024], F32)
    c_py = p3("c_py", [128, 512], F32)
    c_pz = p3("c_pz", [128, 512], F32)
    for half in range(2):
        t0 = half * 1024
        P.dma("sync", yac[:, :].rearrange("p (c d) -> p c d", d=D),
              x1_d.ap().rearrange("(c p) d -> p c d", p=128)[:, half * 8:half * 8 + 8, :])
        P.dma("sync", l2h[:, :].rearrange("p (d t) -> p d t", t=1024),
              l2fm_d.ap().rearrange("d p t -> p d t")[:, :, t0:t0 + 1024])
        for e in range(18):
            IT = 11
            for it in range(IT):
                buf = (it % 2) * DT * 128
                if e < 16:
                    w1src = w1t[e * D:(e + 1) * D, it * 128:(it + 1) * 128]
                    w3src = w3t[e * D:(e + 1) * D, it * 128:(it + 1) * 128]
                    w2src = w2t[e * 1408 + it * 128: e * 1408 + (it + 1) * 128, :]
                else:
                    sh = e - 16
                    w1src = ws1t[:, sh * 1408 + it * 128: sh * 1408 + (it + 1) * 128]
                    w3src = ws3t[:, sh * 1408 + it * 128: sh * 1408 + (it + 1) * 128]
                    w2src = ws2t[sh * 1408 + it * 128: sh * 1408 + (it + 1) * 128, :]
                P.dma("gpsimd", w1b[:, buf:buf + DT * 128].rearrange("p (d f) -> p d f", f=128),
                      w1src.rearrange("(d p) f -> p d f", p=128))
                P.dma("gpsimd", w3b[:, buf:buf + DT * 128].rearrange("p (d f) -> p d f", f=128),
                      w3src.rearrange("(d p) f -> p d f", p=128))
                P.dma("gpsimd", w2b[:, it * D:(it + 1) * D], w2src)
                for nt in range(2):
                    for kt in range(DT):
                        P.mm(c_pa[:, nt * 512:(nt + 1) * 512],
                             w1b[:, buf + kt * 128:buf + (kt + 1) * 128],
                             l2h[:, kt * 1024 + nt * 512:kt * 1024 + (nt + 1) * 512],
                             start=(kt == 0), stop=(kt == DT - 1))
                    for kt in range(DT):
                        P.mm(c_pb[:, nt * 512:(nt + 1) * 512],
                             w3b[:, buf + kt * 128:buf + (kt + 1) * 128],
                             l2h[:, kt * 1024 + nt * 512:kt * 1024 + (nt + 1) * 512],
                             start=(kt == 0), stop=(kt == DT - 1))
                P.act(sil[:, :], c_pa[:, :], AF.Silu)
                P.tt(gbf[:, it * 1024:(it + 1) * 1024], sil[:, :], c_pb[:, :],
                     ALU.mult)
            pp = [c_py, c_pz]
            for tq in range(8):
                oc = half * 8 + tq
                for nb in range(4):
                    ps = pp[nb % 2]
                    for it in range(IT):
                        P.mm(ps[:, :], gbf[:, it * 1024 + tq * 128:it * 1024 + tq * 128 + 128],
                             w2b[:, it * D + nb * 512:it * D + (nb + 1) * 512],
                             start=(it == 0), stop=(it == IT - 1))
                    ysl = yac[:, tq * D + nb * 512:tq * D + (nb + 1) * 512]
                    if e < 16:
                        P.stt(ysl, ps[:, :], comb[:, oc * 16 + e:oc * 16 + e + 1],
                              ysl, ALU.mult, ALU.add)
                    else:
                        P.stt(ysl, ps[:, :], 1.0, ysl, ALU.mult, ALU.add)
        P.dma("sync",
              yout[half * 1024:(half + 1) * 1024, :].rearrange("(c p) d -> p c d", p=128),
              yac[:, :].rearrange("p (c d) -> p c d", d=D))
    P.barrier()
    sC.close()

    for name in sorted(TAPS):
        t = scratch[name]
        shp = list(t.shape)
        n0 = int(np.prod(shp[:-1]))
        flat_in = t.ap()
        if len(shp) == 3:
            flat_in = flat_in.rearrange("a b c -> (a b) c")
        o = nc.dram_tensor("tap_" + name, [n0, shp[-1]], t.dtype,
                           kind="ExternalOutput").ap()
        with nc.sbuf_tensor("tapb_" + name, [128, shp[-1]], t.dtype) as tb:
            for r in range(0, n0, 128):
                P.dma("sync", tb[:, :], flat_in[r:r + 128, :])
                P.dma("sync", o[r:r + 128, :], tb[:, :])

    with nc.Block() as blk:
        P.emit(nc, blk, outer)
    outer.close()
    return nc



_CACHE = {}


def _consts():
    idb = np.eye(128, dtype=BF)
    idf = np.eye(128, dtype=np.float32)
    sel = np.zeros((16, 2048), np.float32)
    for i in range(16):
        sel[i, i * 128:(i + 1) * 128] = 1.0
    onek = np.zeros((128, 128), np.float32)
    for j in range(8):
        onek[:, j * 16 + 2 * j] = 1.0
        onek[:, j * 16 + 2 * j + 1] = 1.0
    iotas = np.zeros((128, 40), np.float32)
    iotas[:, 0:16] = np.arange(16)[None, :]
    iotas[:, 16:32] = np.arange(16)[None, :] + 65536.0
    iotas[:, 32:36] = np.arange(4)[None, :]
    iotas[:, 36:40] = np.arange(4)[None, :] + 65536.0
    masks = np.zeros((128, 256), np.float32)
    masks[:, 0:128] = np.tril(np.ones((128, 128), np.float32), -1)
    masks[:, 128:256] = np.triu(np.ones((128, 128), np.float32), 0)
    return dict(idb=idb, idf=idf, sel16=sel, onek=onek.astype(BF),
                iotas=iotas, masks=masks.astype(BF))


def kernel(**inputs):
    x = np.asarray(inputs["L0_attn_hidden_states"])
    Wq = np.asarray(inputs["L0_attn_W_qkvz"])
    Wba = np.asarray(inputs["L0_attn_W_ba"])
    conv_w = np.asarray(inputs["L0_attn_conv_w"])
    dt_bias = np.asarray(inputs["L0_attn_dt_bias"])
    A_log = np.asarray(inputs["L0_attn_A_log"])
    nw = np.asarray(inputs["L0_attn_norm_weight"])
    Wo = np.asarray(inputs["L0_attn_W_out"])
    gate_w = np.asarray(inputs["L0_ffn_gate_w"])
    W1 = np.asarray(inputs["L0_ffn_W1"])
    W2 = np.asarray(inputs["L0_ffn_W2"])
    W3 = np.asarray(inputs["L0_ffn_W3"])
    Ws1 = np.asarray(inputs["L0_ffn_Ws1"])
    Ws2 = np.asarray(inputs["L0_ffn_Ws2"])
    Ws3 = np.asarray(inputs["L0_ffn_Ws3"])

    qc = np.concatenate([np.arange(j * 768, j * 768 + 128) for j in range(8)])
    kc = np.concatenate([np.arange(j * 768 + 128, j * 768 + 256) for j in range(8)])
    vc = np.concatenate([np.arange(j * 768 + 256, j * 768 + 512) for j in range(8)])
    zc = np.concatenate([np.arange(j * 768 + 512, j * 768 + 768) for j in range(8)])
    bc = np.concatenate([np.arange(j * 4, j * 4 + 2) for j in range(8)])
    ac = np.concatenate([np.arange(j * 4 + 2, j * 4 + 4) for j in range(8)])
    wcat = np.ascontiguousarray(
        np.concatenate([Wq[:, qc], Wq[:, kc], Wq[:, vc], Wq[:, zc],
                        Wba[:, bc], Wba[:, ac]], axis=1))
    convd = np.zeros((4, 32, 128, 128), np.float32)
    idx = np.arange(128)
    for j in range(4):
        for ct in range(32):
            convd[j, ct, idx, idx] = conv_w[ct * 128 + idx, j]
    consts = _consts()
    shared = dict(
        wcat=wcat,
        convd=np.ascontiguousarray(convd.transpose(2, 0, 1, 3).reshape(128, 4 * 32 * 128)).astype(BF),
        alog=A_log.reshape(16, 1).astype(np.float32),
        dtb=dt_bias.reshape(16, 1).astype(np.float32),
        nwbi=np.broadcast_to(np.tile(nw, 16), (128, 2048)).astype(BF).copy(),
        wout=Wo,
        gwt=np.ascontiguousarray(gate_w.T),
        w1t=np.ascontiguousarray(W1.transpose(0, 2, 1)).reshape(16 * 2048, 1408),
        w3t=np.ascontiguousarray(W3.transpose(0, 2, 1)).reshape(16 * 2048, 1408),
        w2t=np.ascontiguousarray(W2.transpose(0, 2, 1)).reshape(16 * 1408, 2048),
        ws1t=np.ascontiguousarray(Ws1.T),
        ws3t=np.ascontiguousarray(Ws3.T),
        ws2t=np.ascontiguousarray(Ws2.T),
        **consts,
    )
    in_maps = []
    for c in range(8):
        b, h = c // 2, c % 2
        xeb = np.zeros((EXT, D), np.float32)
        m = np.zeros(EXT, np.float32)
        if h == 1:
            xeb[512:2560] = x[b, 0:2048]
            m[512:2560] = 1.0
        xeb[2560:] = x[b, h * 2048:(h + 1) * 2048]
        m[2560:] = 1.0
        mhb = np.broadcast_to(m[None, :], (16, EXT)).copy()
        in_maps.append(dict(shared, xe=xeb, mh=mhb))

    if "nc" not in _CACHE:
        _CACHE["nc"] = build_nc()
    nc = _CACHE["nc"]
    res = run_bass_kernel_spmd(nc, in_maps, core_ids=list(range(8)))
    out = np.zeros((4, 4096, 2048), np.float32)
    for c in range(8):
        b, h = c // 2, c % 2
        out[b, h * 2048:(h + 1) * 2048] = res.results[c]["yout"]
    return out
